# version 33
# baseline (speedup 1.0000x reference)
import numpy as np
import ml_dtypes
import concourse.bass as bass
import concourse.mybir as mybir
from concourse.bass_utils import run_bass_kernel_spmd

F32 = mybir.dt.float32
BF16 = mybir.dt.bfloat16
U8 = mybir.dt.uint8
AF = mybir.ActivationFunctionType
ALU = mybir.AluOpType
AX = mybir.AxisListType

B, S, DM = 2, 4096, 2048
HID = 5632
NCORE = 8
TOK = 1024
NT = 8
NK = 16
NE = 48
HB = 44
NEC = 4
G3 = 4
HRING = 6
LN_EPS = 1e-5
FM_EPS = 1e-12
Q_EPS = 1e-8
NGS = 33
NGSM = 11
NW = float(HID * DM)

ENGS = ("tensor", "vector", "scalar", "gpsimd", "sync")


class Sem:
    def __init__(self, nc, name):
        self.h = nc.alloc_semaphore(name)
        self.n = 0


def build_nc(dbg=False):
    nc = bass.Bass("TRN2", target_bir_lowering=False)

    x_img = nc.dram_tensor("x_img", [128, NT * DM], F32, kind="ExternalInput")
    wqkv_img = nc.dram_tensor("wqkv_img", [128, NE * NK * 128], F32, kind="ExternalInput")
    bqkv_img = nc.dram_tensor("bqkv_img", [128, NE], F32, kind="ExternalInput")
    wout_img = nc.dram_tensor("wout_img", [128, NEC * NK * 512], F32, kind="ExternalInput")
    bout_img = nc.dram_tensor("bout_img", [1, DM], F32, kind="ExternalInput")
    w1_img = nc.dram_tensor("w1_img", [128, HB * NK * 128], F32, kind="ExternalInput")
    w2_img = nc.dram_tensor("w2_img", [128, HB * NK * 128], F32, kind="ExternalInput")
    w3_img = nc.dram_tensor("w3_img", [128, HB * 4 * 512], F32, kind="ExternalInput")
    gslab_img = nc.dram_tensor("gslab_img", [128, NGS * 1024], F32, kind="ExternalInput")
    cos_img = nc.dram_tensor("cos_img", [128, 4 * TOK], F32, kind="ExternalInput")
    sin_img = nc.dram_tensor("sin_img", [128, 4 * TOK], F32, kind="ExternalInput")
    eye_in = nc.dram_tensor("eye_in", [128, 128], BF16, kind="ExternalInput")
    bmask_in = nc.dram_tensor("bmask_in", [128, 32], F32, kind="ExternalInput")
    y_img = nc.dram_tensor("y_img", [128, NT * DM], F32, kind="ExternalOutput")
    x1_dram = nc.dram_tensor("x1_dram", [128, NT * DM], F32, kind="Internal")
    cc_in = nc.dram_tensor("cc_in", [128, 36], F32, kind="Internal")
    cc_out = nc.dram_tensor("cc_out", [128, 36], F32, kind="Internal")
    if dbg:
        d_ccst = nc.dram_tensor("d_ccst", [128, 36], F32, kind="ExternalOutput")
        d_asb = nc.dram_tensor("d_asb", [128, 16], F32, kind="ExternalOutput")
        d_crep = nc.dram_tensor("d_crep", [128, TOK], F32, kind="ExternalOutput")
        d_kt = nc.dram_tensor("d_kt", [128, NK * TOK], BF16, kind="ExternalOutput")
        d_yt = nc.dram_tensor("d_yt", [128, NK * TOK], BF16, kind="ExternalOutput")
        d_attn = nc.dram_tensor("d_attn", [128, NT * DM], BF16, kind="ExternalOutput")
        d_x1 = nc.dram_tensor("d_x1", [128, NT * DM], F32, kind="ExternalOutput")
        d_ffn = nc.dram_tensor("d_ffn", [128, NT * DM], F32, kind="ExternalOutput")
        d_thr = nc.dram_tensor("d_thr", [128, 4], F32, kind="ExternalOutput")
        d_xft = nc.dram_tensor("d_xft", [128, NK * TOK], BF16, kind="ExternalOutput")

    def carve(region, off, shape, dtype):
        nb = shape[1] * (4 if dtype == F32 else 2)
        return region[0:shape[0], off:off + nb].bitcast(dtype)

    arena = [nc.alloc_sbuf_tensor(f"arena{t}", [128, 2048], F32) for t in range(NT)]
    regBK = nc.alloc_sbuf_tensor("regBK", [128, 65536], U8)
    regY = nc.alloc_sbuf_tensor("regY", [128, 32768], U8)
    regC = nc.alloc_sbuf_tensor("regC", [128, 12288], U8)
    regW = nc.alloc_sbuf_tensor("regW", [128, 24576], U8)
    scr_a = nc.alloc_sbuf_tensor("scr_a", [128, 1024], F32)
    xnb = nc.alloc_sbuf_tensor("xnb", [128, 2048], BF16)
    xnbf = xnb[:, :].bitcast(F32)[:, 0:1024]

    BT = [carve(regBK, k * 2048, [128, TOK], BF16) for k in range(NK)]
    KT = [carve(regBK, 32768 + i * 2048, [128, TOK], BF16) for i in range(NK)]
    attnres = [carve(regBK, t * 4096, [128, 2048], BF16) for t in range(NT)]
    Wo = [carve(regBK, 32768 + p * 16384, [128, 8192], BF16) for p in range(2)]
    q1r = [carve(regBK, p * 4096, [128, 2048], BF16) for p in range(2)]
    q2r = [carve(regBK, 8192 + p * 4096, [128, 2048], BF16) for p in range(2)]
    w3q = [carve(regBK, 16384 + j * 4096, [128, 2048], BF16) for j in range(4)]
    hid = [carve(regBK, 32768 + h * 2048, [128, TOK], BF16) for h in range(HRING)]
    x1rr = [carve(regBK, 49152 + p * 8192, [128, 2048], F32) for p in range(2)]
    boutF = carve(regBK, 49152, [1, 2048], F32)

    YT = [carve(regY, i * 2048, [128, TOK], BF16) for i in range(NK)]
    gstage = [carve(regY, p * 4096, [128, 1024], F32) for p in range(2)]

    cosb = carve(regC, 0, [128, TOK], F32)
    sinb = carve(regC, 4096, [128, TOK], F32)
    bout_bf = carve(regC, 8192, [1, 2048], BF16)
    wstage = [carve(regC, 0, [128, 1024], F32),
              carve(regC, 4096, [128, 1024], F32),
              carve(regW, 16384, [128, 1024], F32),
              carve(regW, 20480, [128, 1024], F32)]

    Wq = [carve(regW, p * 4096, [128, 2048], BF16) for p in range(2)]
    sstage = [carve(regW, 8192 + p * 4096, [128, 1024], F32) for p in range(2)]
    c_rep = carve(regW, 16384, [128, TOK], F32)
    xfb2 = carve(regW, 0, [128, 2048], BF16)
    qsgn = [carve(regW, 20480, [128, 2048], BF16), xnb,
            xfb2, carve(regW, 4096, [128, 2048], BF16)]
    xnb2 = qsgn[0]

    eps_ln = nc.alloc_sbuf_tensor("eps_ln", [128, 1], F32)
    ones_col = nc.alloc_sbuf_tensor("ones_col", [128, 1], F32)
    ones_cb = nc.alloc_sbuf_tensor("ones_cb", [128, 1], BF16)
    ones_rf = nc.alloc_sbuf_tensor("ones_rf", [1, 128], F32)
    ones_rb = nc.alloc_sbuf_tensor("ones_rb", [1, 128], BF16)
    eye = nc.alloc_sbuf_tensor("eye", [128, 128], BF16)
    bqkv = nc.alloc_sbuf_tensor("bqkv", [128, NE], F32)
    bmask = nc.alloc_sbuf_tensor("bmask", [128, 32], F32)
    ccs = nc.alloc_sbuf_tensor("ccs", [128, 36], F32)
    ccst = nc.alloc_sbuf_tensor("ccst", [128, 36], F32)
    A_sb = nc.alloc_sbuf_tensor("A_sb", [128, 16], F32)
    gcols = nc.alloc_sbuf_tensor("gcols", [128, NGS], F32)
    gthr = nc.alloc_sbuf_tensor("gthr", [128, 4], F32)
    nthr = nc.alloc_sbuf_tensor("nthr", [128, 4], F32)
    thr_row = nc.alloc_sbuf_tensor("thr_row", [1, 4], F32)
    ln1p = nc.alloc_sbuf_tensor("ln1p", [128, 2 * NT], F32)
    ln3p = nc.alloc_sbuf_tensor("ln3p", [128, 2 * NT], F32)
    negc2 = nc.alloc_sbuf_tensor("negc2", [128, NT], F32)
    bnst = nc.alloc_sbuf_tensor("bnst", [128, 24], F32)
    ctmp = nc.alloc_sbuf_tensor("ctmp", [128, 8], F32)

    P = [nc.alloc_psum_tensor(f"p{i}", [128, 512], F32) for i in range(8)]
    pt4 = [P[2 + i][:, :].bitcast(BF16)[:, 0:128] for i in range(4)]

    SX = Sem(nc, "SX")
    SXS = Sem(nc, "SXS")
    SXG = Sem(nc, "SXG")
    SCS = Sem(nc, "SCS")
    SQ = [Sem(nc, "SQ0"), Sem(nc, "SQ1")]
    SG = [Sem(nc, "SG0"), Sem(nc, "SG1")]
    SF = [Sem(nc, "SF0"), Sem(nc, "SF1")]
    SBO = Sem(nc, "SBO")
    SSP = Sem(nc, "SSP")
    SXR = Sem(nc, "SXR")
    SYO = Sem(nc, "SYO")
    SCC = Sem(nc, "SCC")
    SCCD = Sem(nc, "SCCD")
    SDBG = Sem(nc, "SDBG") if dbg else None
    TMM = Sem(nc, "TMM")
    VC = Sem(nc, "VC")
    AC = Sem(nc, "AC")
    PC = Sem(nc, "PC")

    prog = {e: [] for e in ENGS}
    SELF = {"vector": VC, "scalar": AC, "gpsimd": PC}

    def WT(eng, sem, val):
        if val and val > 0:
            prog[eng].append(lambda e, s=sem.h, v=val: e.wait_ge(s, v))

    def EM(eng, fn, inc=None, k=1, dma=False):
        s = SELF.get(eng)
        if s is not None and not dma:
            WT(eng, s, s.n)
            if inc is None:
                inc, k = s, 1
            assert inc is s, f"compute op on {eng} must inc its engine sem"
        if inc is not None:
            inc.n += k
            prog[eng].append(lambda e, f=fn, sh=inc.h, K=k: f(e).then_inc(sh, K))
            return inc.n
        prog[eng].append(fn)
        return None

    EM("sync", lambda e: e.dma_start(out=eye[:, :], in_=eye_in[:, :]), inc=SX, k=16)
    EM("sync", lambda e: e.dma_start(out=bqkv[:, :], in_=bqkv_img[:, :]), inc=SX, k=16)
    EM("sync", lambda e: e.dma_start(out=bmask[:, :], in_=bmask_in[:, :]), inc=SX, k=16)
    eye_done = SX.n
    x_done = {}
    xq = [("sync", SX), ("scalar", SXS), ("gpsimd", SXG)]
    for t in range(NT):
        gates = []
        for h in range(2):
            src = x_img[:, t * 2048 + h * 1024: t * 2048 + (h + 1) * 1024]
            qn, sem = xq[(2 * t + h) % 3]
            EM(qn, lambda e, d=arena[t][:, h * 1024:(h + 1) * 1024], s=src:
               e.dma_start(out=d, in_=s), inc=sem, k=16, dma=True)
            gates.append((sem, sem.n))
        x_done[t] = gates
    x_all = (SX.n, SXS.n, SXG.n)

    cs_load = {}
    rope_done_vc = {}

    def emit_cs_load(j, gate_vc):
        WT("scalar", VC, gate_vc)
        i = j % 4
        EM("scalar", lambda e, s=cos_img[:, i * TOK:(i + 1) * TOK]:
           e.dma_start(out=cosb[:, :], in_=s), inc=SCS, k=16, dma=True)
        EM("scalar", lambda e, s=sin_img[:, i * TOK:(i + 1) * TOK]:
           e.dma_start(out=sinb[:, :], in_=s), inc=SCS, k=16, dma=True)
        cs_load[j] = SCS.n

    emit_cs_load(0, 0)

    qdma_done = {}
    qcv_done = {}

    def emit_qkv_dma(n):
        et, q = n // 2, n % 2
        src = wqkv_img[:, et * 2048 + q * 1024: et * 2048 + (q + 1) * 1024]
        WT("sync", AC, qcv_done.get(n - 2, 0))
        EM("sync", lambda e, d=sstage[n % 2], s=src: e.dma_start(out=d[:, :], in_=s),
           inc=SQ[n % 2], k=16)
        qdma_done[n] = SQ[n % 2].n

    emit_qkv_dma(0)
    emit_qkv_dma(1)

    EM("vector", lambda e: e.memset(eps_ln[:, :], LN_EPS))
    EM("vector", lambda e: e.memset(ones_col[:, :], 1.0))
    EM("vector", lambda e: e.memset(ones_cb[:, :], 1.0))
    EM("vector", lambda e: e.memset(ones_rf[:, :], 1.0))
    EM("vector", lambda e: e.memset(ones_rb[:, :], 1.0))
    init_vc = EM("vector", lambda e: e.memset(ccs[:, 32:36], 0.0), inc=VC)

    gamma_q = list(range(NGS))
    gamma_red = {}

    def emit_gamma_units(nu):
        for _ in range(nu):
            if not gamma_q:
                return
            gi = gamma_q.pop(0)
            src = gslab_img[:, gi * 1024:(gi + 1) * 1024]
            if gi < 2:
                WT("gpsimd", SX, x_all[0])
                WT("gpsimd", SXS, x_all[1])
                WT("gpsimd", SXG, x_all[2])
            if gi >= 2:
                WT("gpsimd", VC, gamma_red[gi - 2])
            EM("gpsimd", lambda e, d=gstage[gi % 2], s=src: e.dma_start(out=d[:, :], in_=s),
               inc=SG[gi % 2], k=16, dma=True)
            WT("vector", SG[gi % 2], SG[gi % 2].n)
            gamma_red[gi] = EM(
                "vector",
                lambda e, g=gstage[gi % 2], c=gcols[:, gi:gi + 1]:
                    e.tensor_reduce(c, g[:, :], AX.X, ALU.add, apply_absolute_value=True),
                inc=VC)

    pt_evac = {0: 0, 1: 0, 2: 0, 3: 0}

    def emit_transposes(src_bf, t, gate_vc, dst, banks=(0, 1, 2, 3)):
        first = True
        for k in range(NK):
            bkn = banks[k % len(banks)]
            pt = pt4[bkn]
            if first:
                WT("tensor", VC, gate_vc)
                first = False
            WT("tensor", AC, pt_evac[bkn])
            tc_ = EM("tensor",
                     lambda e, p=pt, s=src_bf[:, k * 128:(k + 1) * 128]:
                     e.transpose(p, s, eye[:, :]),
                     inc=TMM)
            WT("scalar", TMM, tc_)
            pt_evac[bkn] = EM("scalar",
                              lambda e, d=dst[k][:, t * 128:(t + 1) * 128], p=pt:
                              e.copy(d, p), inc=AC)
        return TMM.n

    def emit_ln_stats(t, src, pair):
        for i in range(4):
            EM("vector", lambda e, d=bnst[:, i * 6:(i + 1) * 6],
               s=src[:, i * 512:(i + 1) * 512]: e.bn_stats(d, s))
        g = EM("vector", lambda e: e.bn_aggr(pair, bnst[:, :]), inc=VC)
        WT("scalar", VC, g)
        st = EM("scalar", lambda e: e.activation(pair[:, 1:2], pair[:, 1:2], AF.Sqrt,
                                                 bias=eps_ln[:, :]), inc=AC)
        WT("vector", AC, st)
        return EM("vector", lambda e: e.reciprocal(pair[:, 1:2], pair[:, 1:2]), inc=VC)

    tp_done = {}
    p0_evac_t = {}
    for t in range(NT + 1):
        if t < NT:
            for sem, v in x_done[t]:
                WT("vector", sem, v)
                WT("scalar", sem, v)
            emit_ln_stats(t, arena[t], ln1p[:, 2 * t:2 * t + 2])
        if t >= 1:
            u = t - 1
            buf = xnb if u % 2 == 0 else xnb2
            if u == 0:
                WT("tensor", SX, eye_done)
            if u > 1:
                WT("vector", TMM, tp_done[u - 2])
            xnv = EM("vector", lambda e, x=arena[u], m=ln1p[:, 2 * u:2 * u + 1],
                     r=ln1p[:, 2 * u + 1:2 * u + 2], b=buf:
                     e.tensor_scalar(b[:, :], x[:, :], m, r, ALU.subtract, ALU.mult), inc=VC)
            tp_done[u] = emit_transposes(buf, u, xnv, BT)
            p0_evac_t[u] = (AC.n, VC.n)
    p0_tp_all = TMM.n
    p0_evac_all = (AC.n, VC.n)

    mm_group = {}
    bank_free = {0: init_vc, 1: init_vc, 2: init_vc, 3: init_vc}
    rope_k = {}
    rope_q = {}
    ss_mm = {}
    ss_cnt = [0]
    ttr_done = {}
    cc_back = [0]
    athr_vc = [0]
    bout_cv = [0]
    crep_vc = [0]

    def emit_qkv_convert(et):
        if et >= 2:
            WT("scalar", TMM, mm_group.get((et - 2, 1), 0))
        for q in range(2):
            n = 2 * et + q
            WT("scalar", SQ[n % 2], qdma_done[n])
            src = sstage[n % 2]
            dst = Wq[et % 2][:, q * 1024:(q + 1) * 1024]
            qcv_done[n] = EM("scalar", lambda e, s=src, d=dst: e.copy(d, s[:, :]),
                             inc=AC)
            if n + 2 < 2 * NE:
                emit_qkv_dma(n + 2)

    def emit_rope(i, tiles, store, phase):
        a_ap, b_ap = tiles[i][:, :], tiles[i + 4][:, :]
        j = i if phase == "k" else 4 + i
        WT("vector", SCS, cs_load[j])
        EM("vector", lambda e, a=a_ap: e.tensor_tensor(scr_a[:, :], a, sinb[:, :], ALU.mult))
        EM("vector", lambda e, a=a_ap: e.tensor_tensor(a, a, cosb[:, :], ALU.mult))
        EM("vector", lambda e, b2=b_ap: e.tensor_tensor(xnbf[:, :], b2, sinb[:, :], ALU.mult))
        EM("vector", lambda e, b2=b_ap: e.tensor_tensor(b2, b2, cosb[:, :], ALU.mult))
        EM("vector", lambda e, a=a_ap: e.tensor_tensor(a, a, xnbf[:, :], ALU.subtract))
        store[i] = EM("vector", lambda e, b2=b_ap: e.tensor_tensor(b2, b2, scr_a[:, :], ALU.add),
                      inc=VC)
        if j + 1 < 8:
            emit_cs_load(j + 1, store[i])

    def emit_ksq(j):
        dump = xnb[:, 0:1024] if ss_cnt[0] % 2 == 0 else xnb[:, 1024:2048]
        WT("scalar", VC, evac_vc.get(j, 0))
        if j < 8:
            WT("scalar", VC, rope_k.get(3, 0))
        if ss_cnt[0] >= 2:
            WT("scalar", TMM, ss_mm[ss_cnt[0] - 2])
        sq = EM("scalar", lambda e, s=KT[j], d=dump:
                e.activation(d, s[:, :], AF.Square), inc=AC)
        WT("tensor", AC, sq)
        for tc in range(2):
            st_ = (ss_cnt[0] == 0)
            sp = (ss_cnt[0] == NK - 1)
            cnt = EM("tensor",
                     lambda e, p=P[4 + tc][0:1, 0:512],
                     r=dump[:, tc * 512:(tc + 1) * 512], s=st_, q=sp:
                     e.matmul(p, ones_cb[:, :], r, start=s, stop=q),
                     inc=TMM)
        ss_mm[ss_cnt[0]] = cnt
        ss_cnt[0] += 1

    ksq_sched = {j: [j] for j in range(8, NK)}
    ksq_sched[9] = [9, 0]
    ksq_sched[10] = [10, 1]
    ksq_sched[11] = [11, 2]
    ksq_sched[12] = [12, 3]
    ksq_sched[13] = [13, 4]
    ksq_sched[14] = [14, 5]
    ksq_sched[15] = [15, 6, 7]

    evac_vc = {}
    emit_qkv_convert(0)
    emit_qkv_convert(1)

    def emit_kgroup(et, tc):
        WT("tensor", AC, qcv_done[2 * et + 1])
        bk = tc
        g_ = p0_evac_t[3] if tc == 0 else p0_evac_all
        WT("tensor", AC, g_[0])
        WT("tensor", VC, g_[1])
        WT("tensor", VC, bank_free[bk])
        last_cnt = 0
        for k in range(NK):
            wh = Wq[et % 2][:, k * 128:(k + 1) * 128]
            xh = BT[k][:, tc * 512:(tc + 1) * 512]
            last_cnt = EM("tensor", lambda e, p=P[bk], a=wh, b2=xh, s=(k == 0),
                          q=(k == NK - 1):
                          e.matmul(p[:, :], a, b2, start=s, stop=q),
                          inc=TMM if k == NK - 1 else None)
        mm_group[(et, tc)] = last_cnt
        WT("vector", TMM, last_cnt)
        dst = KT[et][:, tc * 512:(tc + 1) * 512]
        bank_free[bk] = EM("vector", lambda e, d=dst, p=P[bk], b2=bqkv[:, et:et + 1]:
                           e.tensor_scalar(d, p[:, :], b2, None, ALU.add), inc=VC)

    for et_, tc_ in ((0, 0), (1, 0), (0, 1), (1, 1)):
        emit_kgroup(et_, tc_)
    evac_vc[0] = evac_vc[1] = VC.n
    emit_qkv_convert(2)
    emit_qkv_convert(3)

    for et in range(2, NE):
        if et == 16:
            WT("scalar", TMM, ss_mm[NK - 1])
            c_ac = 0
            for tc in range(2):
                c_ac = EM("scalar",
                          lambda e, p=P[4 + tc][0:1, 0:512], d=scr_a[0:1, tc * 512:(tc + 1) * 512]:
                          e.activation(d, p, AF.Sqrt), inc=AC)
            WT("vector", AC, c_ac)
            EM("vector", lambda e: e.tensor_scalar(scr_a[0:1, :], scr_a[0:1, :],
                                                   FM_EPS, None, ALU.max))
            crow_vc = EM("vector", lambda e: e.reciprocal(scr_a[0:1, :], scr_a[0:1, :]), inc=VC)
            WT("tensor", VC, crow_vc)
            rep_cnt = 0
            for tc in range(2):
                rep_cnt = EM("tensor", lambda e, p=P[6 + tc], r=scr_a[0:1, tc * 512:(tc + 1) * 512]:
                             e.matmul(p[:, :], ones_rf[:, :], r, start=True, stop=True), inc=TMM)
            WT("vector", TMM, rep_cnt)
            for tc in range(2):
                crep_vc[0] = EM("vector", lambda e, p=P[6 + tc], d=c_rep[:, tc * 512:(tc + 1) * 512]:
                                e.tensor_copy(d, p[:, :]), inc=VC)

        if et == 28:
            WT("tensor", VC, gamma_red[NGS - 1])
            WT("tensor", VC, crep_vc[0])
            gs_cnt = EM("tensor", lambda e: e.matmul(P[6][0:1, 0:NGS], ones_col[:, :],
                                                     gcols[:, :], start=True, stop=True), inc=TMM)
            WT("vector", TMM, gs_cnt)
            gv = 0
            for m in range(3):
                gv = EM("vector",
                        lambda e, s=P[6][0:1, m * NGSM:(m + 1) * NGSM], d=ccs[0:1, 32 + m:33 + m]:
                        e.tensor_reduce(d, s, AX.X, ALU.add), inc=VC)


        WT("tensor", AC, qcv_done[2 * et + 1])
        last_cnt = 0
        for tc in range(2):
            bk = tc
            if et == 0:
                g_ = p0_evac_t[3] if tc == 0 else p0_evac_all
                WT("tensor", AC, g_[0])
                WT("tensor", VC, g_[1])
            WT("tensor", VC, bank_free[bk])
            for k in range(NK):
                wh = Wq[et % 2][:, k * 128:(k + 1) * 128]
                xh = BT[k][:, tc * 512:(tc + 1) * 512]
                last_cnt = EM("tensor", lambda e, p=P[bk], a=wh, b2=xh, s=(k == 0),
                              q=(k == NK - 1):
                              e.matmul(p[:, :], a, b2, start=s, stop=q),
                              inc=TMM if k == NK - 1 else None)
            mm_group[(et, tc)] = last_cnt
            WT("vector", TMM, last_cnt)
            if et < 16:
                dst = KT[et][:, tc * 512:(tc + 1) * 512]
                bank_free[bk] = EM("vector",
                                   lambda e, d=dst, p=P[bk], b2=bqkv[:, et:et + 1]:
                                   e.tensor_scalar(d, p[:, :], b2, None, ALU.add), inc=VC)
            elif et < 32:
                i = et - 16
                dst = scr_a[:, tc * 512:(tc + 1) * 512]
                bank_free[bk] = EM("vector",
                                   lambda e, d=dst, p=P[bk], b2=bqkv[:, et:et + 1]:
                                   e.tensor_scalar(d, p[:, :], b2, None, ALU.add),
                                   inc=VC)
            else:
                i = et - 32
                if i == 0 and tc == 0:
                    WT("vector", VC, gamma_red[NGS - 1])
                dst = YT[i][:, tc * 512:(tc + 1) * 512]
                bank_free[bk] = EM("vector",
                                   lambda e, d=dst, p=P[bk], b2=bqkv[:, et:et + 1]:
                                   e.tensor_scalar(d, p[:, :], b2, None, ALU.add), inc=VC)
        evac_vc[et] = VC.n

        if 4 <= et <= 7:
            emit_rope(et - 4, KT, rope_k, "k")
        for j in ksq_sched.get(et, []):
            emit_ksq(j)
        if 16 <= et < 32:
            i = et - 16
            EM("vector", lambda e, k_=KT[i]: e.tensor_tensor(scr_a[:, :], scr_a[:, :],
                                                             k_[:, :], ALU.mult))
            EM("vector", lambda e: e.tensor_tensor(scr_a[:, :], scr_a[:, :],
                                                   c_rep[:, :], ALU.mult))
            ttr_done[i] = EM("vector",
                             lambda e, cc=ccs[:, i:i + 1]:
                             e.tensor_reduce(cc, scr_a[:, :], AX.X, ALU.add),
                             inc=VC)
        if et == 32:
            WT("vector", TMM, 0)
            EM("vector", lambda e: e.tensor_tensor(ccst[:, 0:16], ccs[:, 0:16],
                                                   bmask[:, 0:16], ALU.mult))
            EM("vector", lambda e: e.tensor_tensor(ccst[:, 16:32], ccs[:, 0:16],
                                                   bmask[:, 16:32], ALU.mult))
            stg_vc = EM("vector", lambda e: e.tensor_copy(ccst[:, 32:36], ccs[:, 32:36]), inc=VC)
            WT("gpsimd", VC, stg_vc)
            EM("gpsimd", lambda e: e.dma_start(out=cc_in[:, :], in_=ccst[:, :]),
               inc=SCCD, k=16, dma=True)
            WT("gpsimd", SCCD, SCCD.n)
            EM("gpsimd", lambda e: e.collective_compute(
                "AllReduce", ALU.add, replica_groups=[[0, 1, 2, 3, 4, 5, 6, 7]],
                ins=[cc_in.ap().opt()], outs=[cc_out.ap().opt()]), inc=SCC, k=1, dma=True)
            WT("gpsimd", SCC, 1)
            cc_back[0] = EM("gpsimd", lambda e: e.dma_start(out=ccst[:, :], in_=cc_out[:, :]),
                            inc=SCCD, k=16, dma=True)
            WT("scalar", VC, ttr_done[15])
            EM("scalar", lambda e: e.dma_start(out=boutF[:, :], in_=bout_img[:, :]),
               inc=SBO, k=16, dma=True)
            WT("gpsimd", SBO, SBO.n)
            bout_cv[0] = EM("gpsimd", lambda e: e.tensor_copy(bout_bf[:, :], boutF[:, :]), inc=PC)
        if et == 44:
            WT("vector", SCCD, cc_back[0])
            EM("vector", lambda e: e.tensor_scalar(thr_row[0:1, 0:3], ccst[0:1, 32:35],
                                                   0.5 / NW, 0.5 * Q_EPS, ALU.mult, ALU.add),
               inc=VC)
            WT("tensor", VC, VC.n)
            WT("tensor", TMM, 0)
            th_cnt = EM("tensor", lambda e: e.matmul(P[6][0:128, 0:3], ones_rf[:, :],
                                                     thr_row[0:1, 0:3], start=True, stop=True),
                        inc=TMM)
            WT("vector", TMM, th_cnt)
            EM("vector", lambda e: e.tensor_copy(gthr[:, 0:3], P[6][:, 0:3]))
            athr_vc[0] = EM("vector", lambda e: e.tensor_scalar(nthr[:, 0:3], gthr[:, 0:3],
                                                                -1.0, None, ALU.mult), inc=VC)
        if 36 <= et <= 39:
            emit_rope(et - 36, YT, rope_q, "q")
        emit_qkv_convert(et + 2) if et + 2 < NE else None
        emit_gamma_units(3)

    emit_gamma_units(NGS)
    p1_mm_all = TMM.n
    yscale_vc = VC.n

    if dbg:
        WT("gpsimd", VC, yscale_vc)
        WT("gpsimd", SCCD, cc_back[0])
        EM("gpsimd", lambda e: e.dma_start(out=d_ccst[:, :], in_=ccst[:, :]), inc=SDBG, k=16, dma=True)
        EM("gpsimd", lambda e: e.dma_start(out=d_asb[:, :], in_=A_sb[:, :]), inc=SDBG, k=16, dma=True)
        EM("gpsimd", lambda e: e.dma_start(out=d_crep[:, :], in_=c_rep[:, :]), inc=SDBG, k=16, dma=True)
        WT("gpsimd", VC, athr_vc[0])
        EM("gpsimd", lambda e: e.dma_start(out=d_thr[:, :], in_=gthr[:, :]), inc=SDBG, k=16, dma=True)
        for k in range(NK):
            EM("gpsimd", lambda e, s=KT[k], d=d_kt[:, k * TOK:(k + 1) * TOK]:
               e.dma_start(out=d, in_=s[:, :]), inc=SDBG, k=16, dma=True)
            EM("gpsimd", lambda e, s=YT[k], d=d_yt[:, k * TOK:(k + 1) * TOK]:
               e.dma_start(out=d, in_=s[:, :]), inc=SDBG, k=16, dma=True)
        dbg_p1 = SDBG.n

    WT("vector", SCCD, cc_back[0])
    EM("vector", lambda e: e.tensor_tensor(A_sb[:, :], ccst[:, 0:16], bmask[:, 0:16], ALU.mult))
    EM("vector", lambda e: e.tensor_tensor(ccs[:, 0:16], ccst[:, 16:32], bmask[:, 16:32], ALU.mult))
    a_ext_vc = EM("vector", lambda e: e.tensor_tensor(A_sb[:, :], A_sb[:, :], ccs[:, 0:16],
                                                      ALU.add), inc=VC)

    wo_dma = {}
    wo_cv = {}
    wo_group = {}

    def emit_wo_dma(seq):
        if seq >= NEC * 8:
            return
        src = wout_img[:, seq * 1024:(seq + 1) * 1024]
        if seq in (0, 1):
            WT("sync", VC, rope_q[3])
        if seq in (2, 3):
            WT("sync", VC, ttr_done[15])
        WT("sync", AC, wo_cv.get(seq - 4, 0))
        EM("sync", lambda e, d=wstage[seq % 4], s=src: e.dma_start(out=d[:, :], in_=s),
           inc=SQ[seq % 2], k=16)
        wo_dma[seq] = SQ[seq % 2].n

    def emit_wo_convert(ec):
        p = ec % 2
        if ec < 2:
            WT("scalar", TMM, p1_mm_all)
            WT("scalar", VC, ttr_done[15])
        if ec >= 2:
            WT("scalar", TMM, wo_group.get(ec - 2, 0))
        WT("scalar", VC, a_ext_vc)
        for h in range(8):
            seq = ec * 8 + h
            WT("scalar", SQ[seq % 2], wo_dma[seq])
            src = wstage[seq % 4]
            for j in range(2):
                k = 2 * h + j
                dst = Wo[p][:, h * 1024 + j * 512: h * 1024 + (j + 1) * 512]
                wo_cv[seq] = EM("scalar",
                                lambda e, s=src[:, j * 512:(j + 1) * 512], d=dst,
                                a=A_sb[:, k:k + 1]:
                                e.activation(d, s, AF.Copy, scale=a), inc=AC)
            emit_wo_dma(seq + 2)

    emit_wo_dma(0)
    emit_wo_dma(1)
    emit_wo_convert(0)
    emit_wo_convert(1)

    epi_vc = {}
    x1_spill = {}
    xfv_vc = {}

    def emit_epi_a(t):
        emit_ln_stats(t, attnres[t], ctmp[:, 4:6])
        EM("vector", lambda e: e.tensor_scalar(negc2[:, t:t + 1], ctmp[:, 4:5],
                                               ctmp[:, 5:6], -1.0, ALU.mult, ALU.mult))
        for h in range(2):
            sl = slice(h * 1024, (h + 1) * 1024)
            EM("vector", lambda e, x=arena[t], a=attnres[t], s=sl:
               e.scalar_tensor_tensor(x[:, s], a[:, s], ctmp[:, 5:6], x[:, s],
                                      ALU.mult, ALU.add), inc=VC)
        x1v = VC.n
        WT("scalar", VC, x1v)
        x1_spill[t] = EM("scalar", lambda e, a=arena[t], d=x1_dram[:, t * 2048:(t + 1) * 2048]:
                         e.dma_start(out=d, in_=a[:, :]), inc=SSP, k=16, dma=True)
        epi_vc[t] = VC.n
        if dbg:
            WT("gpsimd", VC, epi_vc[t])
            EM("gpsimd", lambda e, a=attnres[t], d=d_attn[:, t * 2048:(t + 1) * 2048]:
               e.dma_start(out=d, in_=a[:, :]), inc=SDBG, k=16, dma=True)
            EM("gpsimd", lambda e, a=arena[t], d=d_x1[:, t * 2048:(t + 1) * 2048]:
               e.dma_start(out=d, in_=a[:, :]), inc=SDBG, k=16, dma=True)

    for ec in range(NEC):
        p = ec % 2
        WT("tensor", AC, wo_cv[ec * 8 + 7])
        if ec == 0:
            WT("tensor", VC, yscale_vc)
            WT("tensor", PC, bout_cv[0])
        cnt = 0
        for t in range(NT):
            bk = (ec * NT + t) % 4
            WT("tensor", VC, bank_free.get(bk, 0))
            ps = P[bk][:, 0:512]
            EM("tensor", lambda e, p_=ps, r=bout_bf[0:1, ec * 512:(ec + 1) * 512]:
               e.matmul(p_, ones_rb[:, :], r, start=True, stop=False))
            for k in range(NK):
                wh = Wo[p][:, k * 512:(k + 1) * 512]
                yh = YT[k][:, t * 128:(t + 1) * 128]
                cnt = EM("tensor", lambda e, p_=ps, a=yh, b2=wh, q=(k == NK - 1):
                         e.matmul(p_, a, b2, start=False, stop=q),
                         inc=TMM if k == NK - 1 else None)
            WT("vector", TMM, cnt)
            dst = attnres[t][:, ec * 512:(ec + 1) * 512]
            if ec == 0 and t == 0:
                WT("vector", TMM, p1_mm_all)
                if dbg:
                    WT("vector", SDBG, dbg_p1)
            bank_free[bk] = EM("vector",
                               lambda e, d=dst, p_=ps, x=arena[t][:, ec * 512:(ec + 1) * 512],
                               r=ln1p[:, 2 * t + 1:2 * t + 2]:
                               e.scalar_tensor_tensor(d, x, r, p_, ALU.mult, ALU.add), inc=VC)
            if ec == NEC - 1:
                emit_epi_a(t)
                if t < 2:
                    emit_ln_stats(t, arena[t], ln3p[:, 2 * t:2 * t + 2])
                    xfv_vc[t] = EM("vector",
                                   lambda e, x=arena[t], m=ln3p[:, 2 * t:2 * t + 1],
                                   r=ln3p[:, 2 * t + 1:2 * t + 2], b=(xnb, xfb2)[t]:
                                   e.tensor_scalar(b[:, :], x[:, :], m, r,
                                                   ALU.subtract, ALU.mult), inc=VC)
        wo_group[ec] = cnt
        if ec + 2 < NEC:
            emit_wo_convert(ec + 2)
    p2_mm_all = TMM.n
    p2_evac_all = VC.n

    sdbg_epi = SDBG.n if dbg else 0

    xf_tp = {}
    xf_evac_t = {}
    for t in range(NT):
        if t >= 2:
            emit_ln_stats(t, arena[t], ln3p[:, 2 * t:2 * t + 2])
            WT("vector", TMM, xf_tp[t - 2])
            xfv_vc[t] = EM("vector", lambda e, x=arena[t], m=ln3p[:, 2 * t:2 * t + 1],
                           r=ln3p[:, 2 * t + 1:2 * t + 2], b=(xnb, xfb2)[t % 2]:
                           e.tensor_scalar(b[:, :], x[:, :], m, r, ALU.subtract, ALU.mult),
                           inc=VC)
        if t == 2:
            WT("tensor", VC, bank_free[2])
            WT("tensor", VC, bank_free[3])
        xf_tp[t] = emit_transposes((xnb, xfb2)[t % 2], t, xfv_vc[t], YT,
                                   banks=(2, 3) if t < 2 else (0, 1, 2, 3))
        xf_evac_t[t] = (AC.n, VC.n)
    xf_tp_all = TMM.n
    xf_evac_all = (AC.n, VC.n)
    if dbg:
        WT("gpsimd", AC, xf_evac_all[0])
        WT("gpsimd", VC, xf_evac_all[1])
        for k in range(NK):
            EM("gpsimd", lambda e, s=YT[k], d=d_xft[:, k * TOK:(k + 1) * TOK]:
               e.dma_start(out=d, in_=s[:, :]), inc=SDBG, k=16, dma=True)
        sdbg_xft = SDBG.n
        WT("tensor", SDBG, sdbg_xft)

    fdma = {}
    quant = {}
    quant_sig = {}
    mm12 = {}
    m3_group = {}
    hid_done = {}
    sig_done = {}
    ar_evac = {}

    fstage = [sstage[0], sstage[1], wstage[0], wstage[1]]

    def emit_fdma(n):
        if n >= 6 * HB:
            return
        hb, part = n // 6, n % 6
        m, h = part // 2, part % 2
        img = (w1_img, w2_img, w3_img)[m]
        src = img[:, hb * 2048 + h * 1024: hb * 2048 + (h + 1) * 1024]
        slotn = n % 4
        qname = ("sync", "scalar")[n % 2]
        sem = SF[n % 2]
        if n in (0, 1):
            WT(qname, AC, qcv_done[2 * NE - 1])
        if n in (2, 3):
            WT(qname, AC, wo_cv[NEC * 8 - 1])
        WT(qname, AC, quant_sig.get(n - 4, 0))
        EM(qname, lambda e, d=fstage[slotn], s=src: e.dma_start(out=d[:, :], in_=s),
           inc=sem, k=16, dma=True)
        fdma[n] = sem.n

    def emit_quant(n):
        hb, part = n // 6, n % 6
        m, h = part // 2, part % 2
        slotn = n % 4
        if m == 0:
            dst = q1r[hb % 2][:, h * 1024:(h + 1) * 1024]
        elif m == 1:
            dst = q2r[hb % 2][:, h * 1024:(h + 1) * 1024]
        else:
            dst = w3q[hb % G3][:, h * 1024:(h + 1) * 1024]
        if hb < 4:
            if dbg:
                WT("vector", SDBG, sdbg_epi)
            if m == 0:
                WT("vector", VC, epi_vc[hb % 2])
            elif m == 1:
                WT("vector", VC, epi_vc[2 + hb % 2])
            else:
                WT("vector", VC, epi_vc[4 + hb % 4])
        if m < 2 and hb >= 2:
            WT("vector", TMM, mm12.get(hb - 2, 0))
        if m == 2 and hb >= G3:
            WT("vector", TMM, m3_group.get(hb // G3 - 1, 0))
        WT("scalar", SF[n % 2], fdma[n])
        WT("scalar", VC, athr_vc[0])
        if n in (0, 3):
            WT("scalar", TMM, p2_mm_all)
        if n in (1, 2):
            WT("scalar", TMM, xf_tp_all)
        WT("scalar", VC, quant.get(n - 4, 0))
        sg = qsgn[n % 4]
        EM("scalar", lambda e, s=fstage[slotn], d=sg[:, 0:1024], b=nthr[:, m:m + 1]:
           e.sign(d, s[:, :], bias=b), inc=AC)
        quant_sig[n] = EM("scalar",
                          lambda e, s=fstage[slotn], d=sg[:, 1024:2048], b=gthr[:, m:m + 1]:
                          e.sign(d, s[:, :], bias=b), inc=AC)
        WT("vector", AC, quant_sig[n])
        quant[n] = EM(
            "vector",
            lambda e, d=dst, s=sg:
            e.tensor_tensor(d, s[:, 0:1024], s[:, 1024:2048], ALU.add),
            inc=VC)

    for n in range(2):
        emit_fdma(n)
    for hb in range(2):
        for part in range(6):
            n = 6 * hb + part
            emit_quant(n)
            emit_fdma(n + 2)

    x1rd = {}
    for t in range(2):
        WT("scalar", VC, epi_vc[NT - 1])
        WT("scalar", AC, xf_evac_all[0])
        WT("scalar", VC, xf_evac_all[1])
        src = x1_dram[:, t * 2048:(t + 1) * 2048]
        EM("scalar", lambda e, d=x1rr[t % 2], s=src: e.dma_start(out=d[:, :], in_=s),
           inc=SXR, k=16, dma=True)
        x1rd[t] = SXR.n

    def emit_p4(t):
        WT("vector", SXR, x1rd[t])
        emit_ln_stats(t, arena[t], ctmp[:, 6:8])
        for h in range(2):
            sl = slice(h * 1024, (h + 1) * 1024)
            EM("vector", lambda e, a=arena[t], s=sl:
               e.tensor_scalar(scr_a[:, :], a[:, s], ctmp[:, 6:7], ctmp[:, 7:8],
                               ALU.subtract, ALU.mult))
            EM("vector", lambda e, a=arena[t], x=x1rr[t % 2], s=sl, ng=negc2[:, t:t + 1]:
               e.scalar_tensor_tensor(a[:, s], x[:, s], ng, scr_a[:, :],
                                      ALU.add, ALU.add), inc=VC)
        yv = VC.n
        WT("scalar", VC, yv)
        EM("scalar", lambda e, a=arena[t], d=y_img[:, t * 2048:(t + 1) * 2048]:
           e.dma_start(out=d, in_=a[:, :]), inc=SYO, k=16, dma=True)
        if t + 2 < NT:
            WT("scalar", VC, yv)
            src = x1_dram[:, (t + 2) * 2048:(t + 3) * 2048]
            EM("scalar", lambda e, d=x1rr[t % 2], s=src: e.dma_start(out=d[:, :], in_=s),
               inc=SXR, k=16, dma=True)
            x1rd[t + 2] = SXR.n

    spill_gated = set()
    bankpair_last = {0: 0, 1: 0}
    for hb in range(HB):
        hs = hb % HRING
        WT("tensor", VC, quant[6 * hb + 1])
        if hb >= 1:
            WT("tensor", AC, sig_done.get(hb - 1, 0))
        cnt = 0
        for tc in range(2):
            if hb == 0:
                g_ = xf_evac_t[3] if tc == 0 else xf_evac_t[7]
                WT("tensor", AC, g_[0])
                WT("tensor", VC, g_[1])
                WT("tensor", VC, bank_free[tc])
                WT("tensor", VC, bank_free[2 + tc])
            for k in range(NK):
                a = q1r[hb % 2][:, k * 128:(k + 1) * 128]
                xh = YT[k][:, tc * 512:(tc + 1) * 512]
                cnt = EM("tensor", lambda e, p=P[tc], a_=a, b2=xh, s=(k == 0), q=(k == NK - 1):
                         e.matmul(p[:, :], a_, b2, start=s, stop=q),
                         inc=TMM if k == NK - 1 else None)
        mm1_cnt = cnt
        WT("tensor", VC, quant[6 * hb + 3])
        if hb == 0:
            WT("tensor", AC, pt_evac[0])
            WT("tensor", AC, pt_evac[1])
        if hb >= 1:
            WT("tensor", VC, hid_done.get(hb - 1, 0))
        for tc in range(2):
            for k in range(NK):
                a = q2r[hb % 2][:, k * 128:(k + 1) * 128]
                xh = YT[k][:, tc * 512:(tc + 1) * 512]
                cnt = EM("tensor", lambda e, p=P[2 + tc], a_=a, b2=xh, s=(k == 0), q=(k == NK - 1):
                         e.matmul(p[:, :], a_, b2, start=s, stop=q),
                         inc=TMM if k == NK - 1 else None)
        mm12[hb] = cnt
        WT("scalar", TMM, mm1_cnt)
        if hb >= 1:
            WT("scalar", VC, hid_done.get(hb - 1, 0))
        sg = 0
        for tc in range(2):
            sg = EM("scalar", lambda e, p=P[tc], d=scr_a[:, tc * 512:(tc + 1) * 512]:
                    e.activation(d, p[:, :], AF.Sigmoid, scale=0.5), inc=AC)
        sig_done[hb] = sg
        if hb + 1 < HB and hb + 1 >= 2:
            for part in range(4):
                n = 6 * (hb + 1) + part
                emit_quant(n)
                emit_fdma(n + 2)
        WT("vector", AC, sg)
        WT("vector", TMM, mm12[hb])
        if hb < HRING:
            WT("vector", TMM, p2_mm_all)
            if dbg:
                WT("vector", SDBG, sdbg_epi)
        for tc in range(2):
            if hb >= HRING and tc == 0:
                WT("vector", TMM, m3_group.get((hb - HRING) // G3, 0))
            hid_done[hb] = EM("vector",
                              lambda e, d=hid[hs][:, tc * 512:(tc + 1) * 512],
                              s=scr_a[:, tc * 512:(tc + 1) * 512], p=P[2 + tc]:
                              e.scalar_tensor_tensor(d, s, 0.5, p[:, :], ALU.mult, ALU.mult),
                              inc=VC)
        g = hb // G3
        if hb % G3 == G3 - 1 or hb == HB - 1:
            hbs = list(range(g * G3, min((g + 1) * G3, HB)))
            WT("tensor", VC, hid_done[hbs[-1]])
            WT("tensor", VC, quant[6 * hbs[-1] + 5])
            if g == 0:
                WT("tensor", AC, pt_evac[2])
                WT("tensor", AC, pt_evac[3])
            for eh in range(2):
                for t in range(NT):
                    pair = t % 2
                    WT("tensor", VC, bankpair_last[pair])
                    if g == 0 and t not in spill_gated:
                        WT("tensor", SSP, x1_spill[t])
                        if dbg:
                            WT("tensor", SDBG, sdbg_epi)
                        spill_gated.add(t)
                    cnt3 = 0
                    for j in range(2):
                        e_ = 2 * eh + j
                        bank = P[4 + 2 * pair + j]
                        first = True
                        for hbk in hbs:
                            hsl = hbk % HRING
                            lh = hid[hsl][:, t * 128:(t + 1) * 128]
                            rr = w3q[hbk % G3][:, e_ * 512:(e_ + 1) * 512]
                            cnt3 = EM("tensor", lambda e, p=bank, a_=lh, b2=rr, s=first,
                                      q=(hbk == hbs[-1]):
                                      e.matmul(p[:, :], a_, b2, start=s, stop=q),
                                      inc=TMM if hbk == hbs[-1] else None)
                            first = False
                    WT("vector", TMM, cnt3)
                    av = 0
                    for j in range(2):
                        e_ = 2 * eh + j
                        bank = P[4 + 2 * pair + j]
                        dst = arena[t][:, e_ * 512:(e_ + 1) * 512]
                        if g == 0:
                            av = EM("vector", lambda e, d=dst, p=bank:
                                    e.tensor_scalar(d, p[:, :], 0.5, None, ALU.mult), inc=VC)
                        else:
                            av = EM("vector", lambda e, d=dst, p=bank:
                                    e.scalar_tensor_tensor(d, p[:, :], 0.5, d, ALU.mult, ALU.add),
                                    inc=VC)
                    bankpair_last[pair] = av
                    ar_evac[(g, t)] = max(ar_evac.get((g, t), 0), av)
                    if g == (HB - 1) // G3 and eh == 1 and t % 2 == 0:
                        emit_p4(t)
            m3_group[g] = cnt3
        if hb + 1 < HB and hb + 1 >= 2:
            for part in (4, 5):
                n = 6 * (hb + 1) + part
                emit_quant(n)
                emit_fdma(n + 2)

    last_g = (HB - 1) // G3
    if dbg:
        for t in range(NT):
            WT("gpsimd", VC, ar_evac[(last_g, t)])
            EM("gpsimd", lambda e, a=arena[t], d=d_ffn[:, t * 2048:(t + 1) * 2048]:
               e.dma_start(out=d, in_=a[:, :]), inc=SDBG, k=16, dma=True)

    for t in (1, 3, 5, 7):
        WT("vector", VC, ar_evac[((HB - 1) // G3, t)])
        emit_p4(t)
    WT("scalar", SYO, SYO.n)
    WT("scalar", SSP, SSP.n)
    if dbg:
        WT("gpsimd", SDBG, SDBG.n)

    with nc.Block() as blk:
        @blk.tensor
        def _(eng):
            for f in prog["tensor"]:
                f(eng)

        @blk.vector
        def _(eng):
            for f in prog["vector"]:
                f(eng)

        @blk.scalar
        def _(eng):
            for f in prog["scalar"]:
                f(eng)

        @blk.gpsimd
        def _(eng):
            for f in prog["gpsimd"]:
                f(eng)

        @blk.sync
        def _(eng):
            for f in prog["sync"]:
                f(eng)

    return nc


def _permD(n):
    p = np.arange(n)
    p[:1024] = np.concatenate([np.arange(0, 1024, 2), np.arange(1, 1024, 2)])
    return p


def _prep_inputs(X, W_in, b_in, W_out, b_out, w1, w2, w3):
    f32 = np.float32
    pD = _permD(DM)
    order = np.concatenate([2048 + pD, 4096 + pD, 0 + pD])
    WtIn = np.ascontiguousarray(W_in[order, :].T, dtype=f32)
    wqkv = np.ascontiguousarray(WtIn.reshape(NK, 128, NE, 128).transpose(1, 2, 0, 3).reshape(128, -1))
    bq = np.ascontiguousarray(np.asarray(b_in, dtype=f32)[order].reshape(NE, 128).T)
    WoT = np.ascontiguousarray(np.asarray(W_out, dtype=f32).T[pD, :])
    wout = np.ascontiguousarray(WoT.reshape(NK, 128, NEC, 512).transpose(1, 2, 0, 3).reshape(128, -1))
    boutr = np.ascontiguousarray(np.asarray(b_out, dtype=f32).reshape(1, DM))
    w1T = np.ascontiguousarray(np.asarray(w1, dtype=f32).T)
    w2T = np.ascontiguousarray(np.asarray(w2, dtype=f32).T)
    w3T = np.ascontiguousarray(np.asarray(w3, dtype=f32).T)
    w1i = np.ascontiguousarray(w1T.reshape(NK, 128, HB, 128).transpose(1, 2, 0, 3).reshape(128, -1))
    w2i = np.ascontiguousarray(w2T.reshape(NK, 128, HB, 128).transpose(1, 2, 0, 3).reshape(128, -1))
    w3i = np.ascontiguousarray(w3T.reshape(HB, 128, 4, 512).transpose(1, 0, 2, 3).reshape(128, -1))
    eye = np.eye(128, dtype=ml_dtypes.bfloat16)

    invf = 10000.0 ** (-(np.arange(512, dtype=np.float64) * 2.0) / 1024.0)
    xs = np.asarray(X, dtype=f32).reshape(B * S, DM)

    in_maps = []
    GW = NGS * 1024 // 3
    for c in range(NCORE):
        rows = xs[c * TOK:(c + 1) * TOK]
        ximg = np.ascontiguousarray(rows.reshape(NT, 128, DM).transpose(1, 0, 2).reshape(128, -1))
        p0 = (c % 4) * TOK
        ang = np.arange(p0, p0 + TOK, dtype=np.float64)[None, :] * invf[:, None]
        cosi = np.ascontiguousarray(np.cos(ang).astype(f32).reshape(4, 128, TOK)
                                    .transpose(1, 0, 2).reshape(128, -1))
        sini = np.ascontiguousarray(np.sin(ang).astype(f32).reshape(4, 128, TOK)
                                    .transpose(1, 0, 2).reshape(128, -1))
        bmask = np.zeros((128, 32), dtype=f32)
        bmask[:, 16 * (c // 4): 16 * (c // 4) + 16] = 1.0
        gsl = np.ascontiguousarray(np.concatenate(
            [w1i[:, c * GW:(c + 1) * GW], w2i[:, c * GW:(c + 1) * GW],
             w3i[:, c * GW:(c + 1) * GW]], axis=1))
        in_maps.append({
            "x_img": ximg, "wqkv_img": wqkv, "bqkv_img": bq, "wout_img": wout,
            "bout_img": boutr, "w1_img": w1i, "w2_img": w2i, "w3_img": w3i,
            "gslab_img": gsl, "cos_img": cosi, "sin_img": sini,
            "eye_in": eye, "bmask_in": bmask,
        })
    return in_maps


def kernel(L0_attn_X, L0_attn_W_in, L0_attn_b_in, L0_attn_W_out, L0_attn_b_out,
           L0_ffn_w1, L0_ffn_w2, L0_ffn_w3):
    in_maps = _prep_inputs(np.asarray(L0_attn_X), np.asarray(L0_attn_W_in),
                           np.asarray(L0_attn_b_in), np.asarray(L0_attn_W_out),
                           np.asarray(L0_attn_b_out), np.asarray(L0_ffn_w1),
                           np.asarray(L0_ffn_w2), np.asarray(L0_ffn_w3))
    nc = build_nc()
    res = run_bass_kernel_spmd(nc, in_maps, core_ids=list(range(NCORE)))
    parts = []
    for c in range(NCORE):
        yi = np.asarray(res.results[c]["y_img"])
        parts.append(yi.reshape(128, NT, DM).transpose(1, 0, 2).reshape(TOK, DM))
    return np.concatenate(parts, axis=0).reshape(B, S, DM).astype(np.float32)


# version 34
# speedup vs baseline: 1.0294x; 1.0294x over previous
import numpy as np
import ml_dtypes
import concourse.bass as bass
import concourse.mybir as mybir
from concourse.bass_utils import run_bass_kernel_spmd

F32 = mybir.dt.float32
BF16 = mybir.dt.bfloat16
U8 = mybir.dt.uint8
AF = mybir.ActivationFunctionType
ALU = mybir.AluOpType
AX = mybir.AxisListType

B, S, DM = 2, 4096, 2048
HID = 5632
NCORE = 8
TOK = 1024
NT = 8
NK = 16
NE = 48
HB = 44
NEC = 4
G3 = 4
HRING = 6
LN_EPS = 1e-5
FM_EPS = 1e-12
Q_EPS = 1e-8
NGS = 33
NGSM = 11
NW = float(HID * DM)

ENGS = ("tensor", "vector", "scalar", "gpsimd", "sync")


class Sem:
    def __init__(self, nc, name):
        self.h = nc.alloc_semaphore(name)
        self.n = 0


def build_nc(dbg=False):
    nc = bass.Bass("TRN2", target_bir_lowering=False)

    x_img = nc.dram_tensor("x_img", [128, NT * DM], F32, kind="ExternalInput")
    wqkv_img = nc.dram_tensor("wqkv_img", [128, NE * NK * 128], F32, kind="ExternalInput")
    bqkv_img = nc.dram_tensor("bqkv_img", [128, NE], F32, kind="ExternalInput")
    wout_img = nc.dram_tensor("wout_img", [128, NEC * NK * 512], F32, kind="ExternalInput")
    bout_img = nc.dram_tensor("bout_img", [1, DM], F32, kind="ExternalInput")
    w1_img = nc.dram_tensor("w1_img", [128, HB * NK * 128], F32, kind="ExternalInput")
    w2_img = nc.dram_tensor("w2_img", [128, HB * NK * 128], F32, kind="ExternalInput")
    w3_img = nc.dram_tensor("w3_img", [128, HB * 4 * 512], F32, kind="ExternalInput")
    gslab_img = nc.dram_tensor("gslab_img", [128, NGS * 1024], F32, kind="ExternalInput")
    cos_img = nc.dram_tensor("cos_img", [128, 4 * TOK], F32, kind="ExternalInput")
    sin_img = nc.dram_tensor("sin_img", [128, 4 * TOK], F32, kind="ExternalInput")
    eye_in = nc.dram_tensor("eye_in", [128, 128], BF16, kind="ExternalInput")
    bmask_in = nc.dram_tensor("bmask_in", [128, 32], F32, kind="ExternalInput")
    y_img = nc.dram_tensor("y_img", [128, NT * DM], F32, kind="ExternalOutput")
    x1_dram = nc.dram_tensor("x1_dram", [128, NT * DM], F32, kind="Internal")
    cc_in = nc.dram_tensor("cc_in", [128, 36], F32, kind="Internal")
    cc_out = nc.dram_tensor("cc_out", [128, 36], F32, kind="Internal")
    if dbg:
        d_ccst = nc.dram_tensor("d_ccst", [128, 36], F32, kind="ExternalOutput")
        d_asb = nc.dram_tensor("d_asb", [128, 16], F32, kind="ExternalOutput")
        d_crep = nc.dram_tensor("d_crep", [128, TOK], F32, kind="ExternalOutput")
        d_kt = nc.dram_tensor("d_kt", [128, NK * TOK], BF16, kind="ExternalOutput")
        d_yt = nc.dram_tensor("d_yt", [128, NK * TOK], BF16, kind="ExternalOutput")
        d_attn = nc.dram_tensor("d_attn", [128, NT * DM], BF16, kind="ExternalOutput")
        d_x1 = nc.dram_tensor("d_x1", [128, NT * DM], F32, kind="ExternalOutput")
        d_ffn = nc.dram_tensor("d_ffn", [128, NT * DM], F32, kind="ExternalOutput")
        d_thr = nc.dram_tensor("d_thr", [128, 4], F32, kind="ExternalOutput")
        d_xft = nc.dram_tensor("d_xft", [128, NK * TOK], BF16, kind="ExternalOutput")

    def carve(region, off, shape, dtype):
        nb = shape[1] * (4 if dtype == F32 else 2)
        return region[0:shape[0], off:off + nb].bitcast(dtype)

    arena = [nc.alloc_sbuf_tensor(f"arena{t}", [128, 2048], F32) for t in range(NT)]
    regBK = nc.alloc_sbuf_tensor("regBK", [128, 65536], U8)
    regY = nc.alloc_sbuf_tensor("regY", [128, 32768], U8)
    regC = nc.alloc_sbuf_tensor("regC", [128, 12288], U8)
    regW = nc.alloc_sbuf_tensor("regW", [128, 24576], U8)
    scr_a = nc.alloc_sbuf_tensor("scr_a", [128, 1024], F32)
    xnb = nc.alloc_sbuf_tensor("xnb", [128, 2048], BF16)
    xnbf = xnb[:, :].bitcast(F32)[:, 0:1024]

    BT = [carve(regBK, k * 2048, [128, TOK], BF16) for k in range(NK)]
    KT = [carve(regBK, 32768 + i * 2048, [128, TOK], BF16) for i in range(NK)]
    attnres = [carve(regBK, t * 4096, [128, 2048], BF16) for t in range(NT)]
    Wo = [carve(regBK, 32768 + p * 16384, [128, 8192], BF16) for p in range(2)]
    q1r = [carve(regBK, p * 4096, [128, 2048], BF16) for p in range(2)]
    q2r = [carve(regBK, 8192 + p * 4096, [128, 2048], BF16) for p in range(2)]
    w3q = [carve(regBK, 16384 + j * 4096, [128, 2048], BF16) for j in range(4)]
    hid = [carve(regBK, 32768 + h * 2048, [128, TOK], BF16) for h in range(HRING)]
    x1rr = [carve(regBK, 49152 + p * 8192, [128, 2048], F32) for p in range(2)]
    boutF = carve(regBK, 49152, [1, 2048], F32)

    YT = [carve(regY, i * 2048, [128, TOK], BF16) for i in range(NK)]
    gstage = [carve(regY, p * 4096, [128, 1024], F32) for p in range(2)]

    cosb = carve(regC, 0, [128, TOK], F32)
    sinb = carve(regC, 4096, [128, TOK], F32)
    bout_bf = carve(regC, 8192, [1, 2048], BF16)
    wstage = [carve(regC, 0, [128, 1024], F32),
              carve(regC, 4096, [128, 1024], F32),
              carve(regW, 16384, [128, 1024], F32),
              carve(regW, 20480, [128, 1024], F32)]

    Wq = [carve(regW, p * 4096, [128, 2048], BF16) for p in range(2)]
    sstage = [carve(regW, 8192 + p * 4096, [128, 1024], F32) for p in range(2)]
    c_rep = carve(regW, 16384, [128, TOK], F32)
    xfb2 = carve(regW, 0, [128, 2048], BF16)
    qsgn = [carve(regW, 20480, [128, 2048], BF16), xnb,
            xfb2, carve(regW, 4096, [128, 2048], BF16)]
    xnb2 = qsgn[0]

    eps_ln = nc.alloc_sbuf_tensor("eps_ln", [128, 1], F32)
    ones_col = nc.alloc_sbuf_tensor("ones_col", [128, 1], F32)
    ones_cb = nc.alloc_sbuf_tensor("ones_cb", [128, 1], BF16)
    ones_rf = nc.alloc_sbuf_tensor("ones_rf", [1, 128], F32)
    ones_rb = nc.alloc_sbuf_tensor("ones_rb", [1, 128], BF16)
    eye = nc.alloc_sbuf_tensor("eye", [128, 128], BF16)
    bqkv = nc.alloc_sbuf_tensor("bqkv", [128, NE], F32)
    bmask = nc.alloc_sbuf_tensor("bmask", [128, 32], F32)
    ccs = nc.alloc_sbuf_tensor("ccs", [128, 36], F32)
    ccst = nc.alloc_sbuf_tensor("ccst", [128, 36], F32)
    A_sb = nc.alloc_sbuf_tensor("A_sb", [128, 16], F32)
    gcols = nc.alloc_sbuf_tensor("gcols", [128, NGS], F32)
    gthr = nc.alloc_sbuf_tensor("gthr", [128, 4], F32)
    nthr = nc.alloc_sbuf_tensor("nthr", [128, 4], F32)
    thr_row = nc.alloc_sbuf_tensor("thr_row", [1, 4], F32)
    ln1p = nc.alloc_sbuf_tensor("ln1p", [128, 2 * NT], F32)
    ln3p = nc.alloc_sbuf_tensor("ln3p", [128, 2 * NT], F32)
    negc2 = nc.alloc_sbuf_tensor("negc2", [128, NT], F32)
    bnst = nc.alloc_sbuf_tensor("bnst", [128, 24], F32)
    ctmp = nc.alloc_sbuf_tensor("ctmp", [128, 8], F32)

    P = [nc.alloc_psum_tensor(f"p{i}", [128, 512], F32) for i in range(8)]
    pt4 = [P[2 + i][:, :].bitcast(BF16)[:, 0:128] for i in range(4)]

    SX = Sem(nc, "SX")
    SXS = Sem(nc, "SXS")
    SXG = Sem(nc, "SXG")
    SCS = Sem(nc, "SCS")
    SQ = [Sem(nc, "SQ0"), Sem(nc, "SQ1")]
    SG = [Sem(nc, "SG0"), Sem(nc, "SG1")]
    SF = [Sem(nc, "SF0"), Sem(nc, "SF1")]
    SBO = Sem(nc, "SBO")
    SSP = Sem(nc, "SSP")
    SXR = Sem(nc, "SXR")
    SYO = Sem(nc, "SYO")
    SCC = Sem(nc, "SCC")
    SCCD = Sem(nc, "SCCD")
    SDBG = Sem(nc, "SDBG") if dbg else None
    TMM = Sem(nc, "TMM")
    VC = Sem(nc, "VC")
    AC = Sem(nc, "AC")
    PC = Sem(nc, "PC")

    prog = {e: [] for e in ENGS}
    SELF = {"vector": VC, "scalar": AC, "gpsimd": PC}

    def WT(eng, sem, val):
        if val and val > 0:
            prog[eng].append(lambda e, s=sem.h, v=val: e.wait_ge(s, v))

    def EM(eng, fn, inc=None, k=1, dma=False):
        s = SELF.get(eng)
        if s is not None and not dma:
            WT(eng, s, s.n)
            if inc is None:
                inc, k = s, 1
            assert inc is s, f"compute op on {eng} must inc its engine sem"
        if inc is not None:
            inc.n += k
            prog[eng].append(lambda e, f=fn, sh=inc.h, K=k: f(e).then_inc(sh, K))
            return inc.n
        prog[eng].append(fn)
        return None

    EM("sync", lambda e: e.dma_start(out=eye[:, :], in_=eye_in[:, :]), inc=SX, k=16)
    EM("sync", lambda e: e.dma_start(out=bqkv[:, :], in_=bqkv_img[:, :]), inc=SX, k=16)
    EM("sync", lambda e: e.dma_start(out=bmask[:, :], in_=bmask_in[:, :]), inc=SX, k=16)
    eye_done = SX.n
    x_done = {}
    xq = [("sync", SX), ("scalar", SXS), ("gpsimd", SXG)]
    for t in range(NT):
        gates = []
        for h in range(2):
            src = x_img[:, t * 2048 + h * 1024: t * 2048 + (h + 1) * 1024]
            qn, sem = xq[(2 * t + h) % 3]
            EM(qn, lambda e, d=arena[t][:, h * 1024:(h + 1) * 1024], s=src:
               e.dma_start(out=d, in_=s), inc=sem, k=16, dma=True)
            gates.append((sem, sem.n))
        x_done[t] = gates
    x_all = (SX.n, SXS.n, SXG.n)

    cs_load = {}
    rope_done_vc = {}

    def emit_cs_load(j, gate_vc):
        WT("scalar", VC, gate_vc)
        i = j % 4
        EM("scalar", lambda e, s=cos_img[:, i * TOK:(i + 1) * TOK]:
           e.dma_start(out=cosb[:, :], in_=s), inc=SCS, k=16, dma=True)
        EM("scalar", lambda e, s=sin_img[:, i * TOK:(i + 1) * TOK]:
           e.dma_start(out=sinb[:, :], in_=s), inc=SCS, k=16, dma=True)
        cs_load[j] = SCS.n

    emit_cs_load(0, 0)

    qdma_done = {}
    qcv_done = {}

    def emit_qkv_dma(n):
        et, q = n // 2, n % 2
        src = wqkv_img[:, et * 2048 + q * 1024: et * 2048 + (q + 1) * 1024]
        WT("sync", AC, qcv_done.get(n - 2, 0))
        EM("sync", lambda e, d=sstage[n % 2], s=src: e.dma_start(out=d[:, :], in_=s),
           inc=SQ[n % 2], k=16)
        qdma_done[n] = SQ[n % 2].n

    emit_qkv_dma(0)
    emit_qkv_dma(1)

    EM("vector", lambda e: e.memset(eps_ln[:, :], LN_EPS))
    EM("vector", lambda e: e.memset(ones_col[:, :], 1.0))
    EM("vector", lambda e: e.memset(ones_cb[:, :], 1.0))
    EM("vector", lambda e: e.memset(ones_rf[:, :], 1.0))
    EM("vector", lambda e: e.memset(ones_rb[:, :], 1.0))
    init_vc = EM("vector", lambda e: e.memset(ccs[:, 32:36], 0.0), inc=VC)

    gamma_q = list(range(NGS))
    gamma_red = {}

    def emit_gamma_units(nu):
        for _ in range(nu):
            if not gamma_q:
                return
            gi = gamma_q.pop(0)
            src = gslab_img[:, gi * 1024:(gi + 1) * 1024]
            if gi < 2:
                WT("gpsimd", SX, x_all[0])
                WT("gpsimd", SXS, x_all[1])
                WT("gpsimd", SXG, x_all[2])
            if gi >= 2:
                WT("gpsimd", VC, gamma_red[gi - 2])
            EM("gpsimd", lambda e, d=gstage[gi % 2], s=src: e.dma_start(out=d[:, :], in_=s),
               inc=SG[gi % 2], k=16, dma=True)
            WT("vector", SG[gi % 2], SG[gi % 2].n)
            gamma_red[gi] = EM(
                "vector",
                lambda e, g=gstage[gi % 2], c=gcols[:, gi:gi + 1]:
                    e.tensor_reduce(c, g[:, :], AX.X, ALU.add, apply_absolute_value=True),
                inc=VC)

    pt_evac = {0: 0, 1: 0, 2: 0, 3: 0}

    def emit_transposes(src_bf, t, gate_vc, dst, banks=(0, 1, 2, 3)):
        first = True
        for k in range(NK):
            bkn = banks[k % len(banks)]
            pt = pt4[bkn]
            if first:
                WT("tensor", VC, gate_vc)
                first = False
            WT("tensor", AC, pt_evac[bkn])
            tc_ = EM("tensor",
                     lambda e, p=pt, s=src_bf[:, k * 128:(k + 1) * 128]:
                     e.transpose(p, s, eye[:, :]),
                     inc=TMM)
            WT("scalar", TMM, tc_)
            pt_evac[bkn] = EM("scalar",
                              lambda e, d=dst[k][:, t * 128:(t + 1) * 128], p=pt:
                              e.copy(d, p), inc=AC)
        return TMM.n

    def emit_ln_stats(t, src, pair):
        for i in range(4):
            EM("vector", lambda e, d=bnst[:, i * 6:(i + 1) * 6],
               s=src[:, i * 512:(i + 1) * 512]: e.bn_stats(d, s))
        g = EM("vector", lambda e: e.bn_aggr(pair, bnst[:, :]), inc=VC)
        WT("scalar", VC, g)
        st = EM("scalar", lambda e: e.activation(pair[:, 1:2], pair[:, 1:2], AF.Sqrt,
                                                 bias=eps_ln[:, :]), inc=AC)
        WT("vector", AC, st)
        return EM("vector", lambda e: e.reciprocal(pair[:, 1:2], pair[:, 1:2]), inc=VC)

    tp_done = {}
    p0_evac_t = {}
    for t in range(NT + 1):
        if t < NT:
            for sem, v in x_done[t]:
                WT("vector", sem, v)
                WT("scalar", sem, v)
            emit_ln_stats(t, arena[t], ln1p[:, 2 * t:2 * t + 2])
        if t >= 1:
            u = t - 1
            buf = xnb if u % 2 == 0 else xnb2
            if u == 0:
                WT("tensor", SX, eye_done)
            if u > 1:
                WT("vector", TMM, tp_done[u - 2])
            xnv = EM("vector", lambda e, x=arena[u], m=ln1p[:, 2 * u:2 * u + 1],
                     r=ln1p[:, 2 * u + 1:2 * u + 2], b=buf:
                     e.tensor_scalar(b[:, :], x[:, :], m, r, ALU.subtract, ALU.mult), inc=VC)
            tp_done[u] = emit_transposes(buf, u, xnv, BT)
            p0_evac_t[u] = (AC.n, VC.n)
    p0_tp_all = TMM.n
    p0_evac_all = (AC.n, VC.n)

    mm_group = {}
    bank_free = {0: init_vc, 1: init_vc, 2: init_vc, 3: init_vc}
    rope_k = {}
    rope_q = {}
    ss_mm = {}
    ss_cnt = [0]
    ttr_done = {}
    cc_back = [0]
    athr_vc = [0]
    bout_cv = [0]
    crep_vc = [0]

    def emit_qkv_convert(et):
        if et >= 2:
            WT("scalar", TMM, mm_group.get((et - 2, 1), 0))
        for q in range(2):
            n = 2 * et + q
            WT("scalar", SQ[n % 2], qdma_done[n])
            src = sstage[n % 2]
            dst = Wq[et % 2][:, q * 1024:(q + 1) * 1024]
            qcv_done[n] = EM("scalar", lambda e, s=src, d=dst: e.copy(d, s[:, :]),
                             inc=AC)
            if n + 2 < 2 * NE:
                emit_qkv_dma(n + 2)

    def emit_rope(i, tiles, store, phase):
        a_ap, b_ap = tiles[i][:, :], tiles[i + 4][:, :]
        j = i if phase == "k" else 4 + i
        WT("vector", SCS, cs_load[j])
        EM("vector", lambda e, a=a_ap: e.tensor_tensor(scr_a[:, :], a, sinb[:, :], ALU.mult))
        EM("vector", lambda e, a=a_ap: e.tensor_tensor(a, a, cosb[:, :], ALU.mult))
        EM("vector", lambda e, b2=b_ap: e.tensor_tensor(xnbf[:, :], b2, sinb[:, :], ALU.mult))
        EM("vector", lambda e, b2=b_ap: e.tensor_tensor(b2, b2, cosb[:, :], ALU.mult))
        EM("vector", lambda e, a=a_ap: e.tensor_tensor(a, a, xnbf[:, :], ALU.subtract))
        store[i] = EM("vector", lambda e, b2=b_ap: e.tensor_tensor(b2, b2, scr_a[:, :], ALU.add),
                      inc=VC)
        if j + 1 < 8:
            emit_cs_load(j + 1, store[i])

    def emit_ksq(j):
        dump = xnb[:, 0:1024] if ss_cnt[0] % 2 == 0 else xnb[:, 1024:2048]
        if j < 8:
            WT("vector", VC, rope_k.get(3, 0))
        if ss_cnt[0] >= 2:
            WT("vector", TMM, ss_mm[ss_cnt[0] - 2])
        sq = EM("vector", lambda e, s=KT[j], d=dump:
                e.tensor_tensor(d, s[:, :], s[:, :], ALU.mult), inc=VC)
        WT("tensor", VC, sq)
        for tc in range(2):
            st_ = (ss_cnt[0] == 0)
            sp = (ss_cnt[0] == NK - 1)
            cnt = EM("tensor",
                     lambda e, p=P[4 + tc][0:1, 0:512],
                     r=dump[:, tc * 512:(tc + 1) * 512], s=st_, q=sp:
                     e.matmul(p, ones_cb[:, :], r, start=s, stop=q),
                     inc=TMM)
        ss_mm[ss_cnt[0]] = cnt
        ss_cnt[0] += 1

    ksq_sched = {j: [j] for j in range(8, NK)}
    ksq_sched[9] = [9, 0]
    ksq_sched[10] = [10, 1]
    ksq_sched[11] = [11, 2]
    ksq_sched[12] = [12, 3]
    ksq_sched[13] = [13, 4]
    ksq_sched[14] = [14, 5]
    ksq_sched[15] = [15, 6, 7]

    evac_vc = {}
    emit_qkv_convert(0)
    emit_qkv_convert(1)

    def emit_kgroup(et, tc):
        WT("tensor", AC, qcv_done[2 * et + 1])
        bk = tc
        g_ = p0_evac_t[3] if tc == 0 else p0_evac_all
        WT("tensor", AC, g_[0])
        WT("tensor", VC, g_[1])
        WT("tensor", VC, bank_free[bk])
        last_cnt = 0
        for k in range(NK):
            wh = Wq[et % 2][:, k * 128:(k + 1) * 128]
            xh = BT[k][:, tc * 512:(tc + 1) * 512]
            last_cnt = EM("tensor", lambda e, p=P[bk], a=wh, b2=xh, s=(k == 0),
                          q=(k == NK - 1):
                          e.matmul(p[:, :], a, b2, start=s, stop=q),
                          inc=TMM if k == NK - 1 else None)
        mm_group[(et, tc)] = last_cnt
        WT("vector", TMM, last_cnt)
        dst = KT[et][:, tc * 512:(tc + 1) * 512]
        bank_free[bk] = EM("vector", lambda e, d=dst, p=P[bk], b2=bqkv[:, et:et + 1]:
                           e.tensor_scalar(d, p[:, :], b2, None, ALU.add), inc=VC)

    for et_, tc_ in ((0, 0), (1, 0), (0, 1), (1, 1)):
        emit_kgroup(et_, tc_)
    evac_vc[0] = evac_vc[1] = VC.n
    emit_qkv_convert(2)
    emit_qkv_convert(3)

    for et in range(2, NE):
        if et == 16:
            WT("scalar", TMM, ss_mm[NK - 1])
            c_ac = 0
            for tc in range(2):
                c_ac = EM("scalar",
                          lambda e, p=P[4 + tc][0:1, 0:512], d=scr_a[0:1, tc * 512:(tc + 1) * 512]:
                          e.activation(d, p, AF.Sqrt), inc=AC)
            WT("vector", AC, c_ac)
            EM("vector", lambda e: e.tensor_scalar(scr_a[0:1, :], scr_a[0:1, :],
                                                   FM_EPS, None, ALU.max))
            crow_vc = EM("vector", lambda e: e.reciprocal(scr_a[0:1, :], scr_a[0:1, :]), inc=VC)
            WT("tensor", VC, crow_vc)
            rep_cnt = 0
            for tc in range(2):
                rep_cnt = EM("tensor", lambda e, p=P[6 + tc], r=scr_a[0:1, tc * 512:(tc + 1) * 512]:
                             e.matmul(p[:, :], ones_rf[:, :], r, start=True, stop=True), inc=TMM)
            WT("vector", TMM, rep_cnt)
            for tc in range(2):
                crep_vc[0] = EM("vector", lambda e, p=P[6 + tc], d=c_rep[:, tc * 512:(tc + 1) * 512]:
                                e.tensor_copy(d, p[:, :]), inc=VC)

        if et == 28:
            WT("tensor", VC, gamma_red[NGS - 1])
            WT("tensor", VC, crep_vc[0])
            gs_cnt = EM("tensor", lambda e: e.matmul(P[6][0:1, 0:NGS], ones_col[:, :],
                                                     gcols[:, :], start=True, stop=True), inc=TMM)
            WT("vector", TMM, gs_cnt)
            gv = 0
            for m in range(3):
                gv = EM("vector",
                        lambda e, s=P[6][0:1, m * NGSM:(m + 1) * NGSM], d=ccs[0:1, 32 + m:33 + m]:
                        e.tensor_reduce(d, s, AX.X, ALU.add), inc=VC)


        WT("tensor", AC, qcv_done[2 * et + 1])
        last_cnt = 0
        for tc in range(2):
            bk = tc
            if et == 0:
                g_ = p0_evac_t[3] if tc == 0 else p0_evac_all
                WT("tensor", AC, g_[0])
                WT("tensor", VC, g_[1])
            WT("tensor", VC, bank_free[bk])
            for k in range(NK):
                wh = Wq[et % 2][:, k * 128:(k + 1) * 128]
                xh = BT[k][:, tc * 512:(tc + 1) * 512]
                last_cnt = EM("tensor", lambda e, p=P[bk], a=wh, b2=xh, s=(k == 0),
                              q=(k == NK - 1):
                              e.matmul(p[:, :], a, b2, start=s, stop=q),
                              inc=TMM if k == NK - 1 else None)
            mm_group[(et, tc)] = last_cnt
            WT("vector", TMM, last_cnt)
            if et < 16:
                dst = KT[et][:, tc * 512:(tc + 1) * 512]
                bank_free[bk] = EM("vector",
                                   lambda e, d=dst, p=P[bk], b2=bqkv[:, et:et + 1]:
                                   e.tensor_scalar(d, p[:, :], b2, None, ALU.add), inc=VC)
            elif et < 32:
                i = et - 16
                dst = scr_a[:, tc * 512:(tc + 1) * 512]
                bank_free[bk] = EM("vector",
                                   lambda e, d=dst, p=P[bk], b2=bqkv[:, et:et + 1]:
                                   e.tensor_scalar(d, p[:, :], b2, None, ALU.add),
                                   inc=VC)
            else:
                i = et - 32
                if i == 0 and tc == 0:
                    WT("vector", VC, gamma_red[NGS - 1])
                dst = YT[i][:, tc * 512:(tc + 1) * 512]
                bank_free[bk] = EM("vector",
                                   lambda e, d=dst, p=P[bk], b2=bqkv[:, et:et + 1]:
                                   e.tensor_scalar(d, p[:, :], b2, None, ALU.add), inc=VC)
        evac_vc[et] = VC.n

        if 4 <= et <= 7:
            emit_rope(et - 4, KT, rope_k, "k")
        for j in ksq_sched.get(et, []):
            emit_ksq(j)
        if 16 <= et < 32:
            i = et - 16
            EM("vector", lambda e, k_=KT[i]: e.tensor_tensor(scr_a[:, :], scr_a[:, :],
                                                             k_[:, :], ALU.mult))
            EM("vector", lambda e: e.tensor_tensor(scr_a[:, :], scr_a[:, :],
                                                   c_rep[:, :], ALU.mult))
            ttr_done[i] = EM("vector",
                             lambda e, cc=ccs[:, i:i + 1]:
                             e.tensor_reduce(cc, scr_a[:, :], AX.X, ALU.add),
                             inc=VC)
        if et == 32:
            WT("vector", TMM, 0)
            EM("vector", lambda e: e.tensor_tensor(ccst[:, 0:16], ccs[:, 0:16],
                                                   bmask[:, 0:16], ALU.mult))
            EM("vector", lambda e: e.tensor_tensor(ccst[:, 16:32], ccs[:, 0:16],
                                                   bmask[:, 16:32], ALU.mult))
            stg_vc = EM("vector", lambda e: e.tensor_copy(ccst[:, 32:36], ccs[:, 32:36]), inc=VC)
            WT("gpsimd", VC, stg_vc)
            EM("gpsimd", lambda e: e.dma_start(out=cc_in[:, :], in_=ccst[:, :]),
               inc=SCCD, k=16, dma=True)
            WT("gpsimd", SCCD, SCCD.n)
            EM("gpsimd", lambda e: e.collective_compute(
                "AllReduce", ALU.add, replica_groups=[[0, 1, 2, 3, 4, 5, 6, 7]],
                ins=[cc_in.ap().opt()], outs=[cc_out.ap().opt()]), inc=SCC, k=1, dma=True)
            WT("gpsimd", SCC, 1)
            cc_back[0] = EM("gpsimd", lambda e: e.dma_start(out=ccst[:, :], in_=cc_out[:, :]),
                            inc=SCCD, k=16, dma=True)
            WT("scalar", VC, ttr_done[15])
            EM("scalar", lambda e: e.dma_start(out=boutF[:, :], in_=bout_img[:, :]),
               inc=SBO, k=16, dma=True)
            WT("gpsimd", SBO, SBO.n)
            bout_cv[0] = EM("gpsimd", lambda e: e.tensor_copy(bout_bf[:, :], boutF[:, :]), inc=PC)
        if et == 44:
            WT("vector", SCCD, cc_back[0])
            EM("vector", lambda e: e.tensor_scalar(thr_row[0:1, 0:3], ccst[0:1, 32:35],
                                                   0.5 / NW, 0.5 * Q_EPS, ALU.mult, ALU.add),
               inc=VC)
            WT("tensor", VC, VC.n)
            WT("tensor", TMM, 0)
            th_cnt = EM("tensor", lambda e: e.matmul(P[6][0:128, 0:3], ones_rf[:, :],
                                                     thr_row[0:1, 0:3], start=True, stop=True),
                        inc=TMM)
            WT("vector", TMM, th_cnt)
            EM("vector", lambda e: e.tensor_copy(gthr[:, 0:3], P[6][:, 0:3]))
            athr_vc[0] = EM("vector", lambda e: e.tensor_scalar(nthr[:, 0:3], gthr[:, 0:3],
                                                                -1.0, None, ALU.mult), inc=VC)
        if 36 <= et <= 39:
            emit_rope(et - 36, YT, rope_q, "q")
        emit_qkv_convert(et + 2) if et + 2 < NE else None
        emit_gamma_units(3)

    emit_gamma_units(NGS)
    p1_mm_all = TMM.n
    yscale_vc = VC.n

    if dbg:
        WT("gpsimd", VC, yscale_vc)
        WT("gpsimd", SCCD, cc_back[0])
        EM("gpsimd", lambda e: e.dma_start(out=d_ccst[:, :], in_=ccst[:, :]), inc=SDBG, k=16, dma=True)
        EM("gpsimd", lambda e: e.dma_start(out=d_asb[:, :], in_=A_sb[:, :]), inc=SDBG, k=16, dma=True)
        EM("gpsimd", lambda e: e.dma_start(out=d_crep[:, :], in_=c_rep[:, :]), inc=SDBG, k=16, dma=True)
        WT("gpsimd", VC, athr_vc[0])
        EM("gpsimd", lambda e: e.dma_start(out=d_thr[:, :], in_=gthr[:, :]), inc=SDBG, k=16, dma=True)
        for k in range(NK):
            EM("gpsimd", lambda e, s=KT[k], d=d_kt[:, k * TOK:(k + 1) * TOK]:
               e.dma_start(out=d, in_=s[:, :]), inc=SDBG, k=16, dma=True)
            EM("gpsimd", lambda e, s=YT[k], d=d_yt[:, k * TOK:(k + 1) * TOK]:
               e.dma_start(out=d, in_=s[:, :]), inc=SDBG, k=16, dma=True)
        dbg_p1 = SDBG.n

    WT("vector", SCCD, cc_back[0])
    EM("vector", lambda e: e.tensor_tensor(A_sb[:, :], ccst[:, 0:16], bmask[:, 0:16], ALU.mult))
    EM("vector", lambda e: e.tensor_tensor(ccs[:, 0:16], ccst[:, 16:32], bmask[:, 16:32], ALU.mult))
    a_ext_vc = EM("vector", lambda e: e.tensor_tensor(A_sb[:, :], A_sb[:, :], ccs[:, 0:16],
                                                      ALU.add), inc=VC)

    wo_dma = {}
    wo_cv = {}
    wo_group = {}

    def emit_wo_dma(seq):
        if seq >= NEC * 8:
            return
        src = wout_img[:, seq * 1024:(seq + 1) * 1024]
        if seq in (0, 1):
            WT("sync", VC, rope_q[3])
        if seq in (2, 3):
            WT("sync", VC, ttr_done[15])
        WT("sync", AC, wo_cv.get(seq - 4, 0))
        EM("sync", lambda e, d=wstage[seq % 4], s=src: e.dma_start(out=d[:, :], in_=s),
           inc=SQ[seq % 2], k=16)
        wo_dma[seq] = SQ[seq % 2].n

    def emit_wo_convert(ec):
        p = ec % 2
        if ec < 2:
            WT("scalar", TMM, p1_mm_all)
            WT("scalar", VC, ttr_done[15])
        if ec >= 2:
            WT("scalar", TMM, wo_group.get(ec - 2, 0))
        WT("scalar", VC, a_ext_vc)
        for h in range(8):
            seq = ec * 8 + h
            WT("scalar", SQ[seq % 2], wo_dma[seq])
            src = wstage[seq % 4]
            for j in range(2):
                k = 2 * h + j
                dst = Wo[p][:, h * 1024 + j * 512: h * 1024 + (j + 1) * 512]
                wo_cv[seq] = EM("scalar",
                                lambda e, s=src[:, j * 512:(j + 1) * 512], d=dst,
                                a=A_sb[:, k:k + 1]:
                                e.activation(d, s, AF.Copy, scale=a), inc=AC)
            emit_wo_dma(seq + 2)

    emit_wo_dma(0)
    emit_wo_dma(1)
    emit_wo_convert(0)
    emit_wo_convert(1)

    epi_vc = {}
    x1_spill = {}
    xfv_vc = {}

    def emit_epi_a(t):
        emit_ln_stats(t, attnres[t], ctmp[:, 4:6])
        EM("vector", lambda e: e.tensor_scalar(negc2[:, t:t + 1], ctmp[:, 4:5],
                                               ctmp[:, 5:6], -1.0, ALU.mult, ALU.mult))
        for h in range(2):
            sl = slice(h * 1024, (h + 1) * 1024)
            EM("vector", lambda e, x=arena[t], a=attnres[t], s=sl:
               e.scalar_tensor_tensor(x[:, s], a[:, s], ctmp[:, 5:6], x[:, s],
                                      ALU.mult, ALU.add), inc=VC)
        x1v = VC.n
        WT("scalar", VC, x1v)
        x1_spill[t] = EM("scalar", lambda e, a=arena[t], d=x1_dram[:, t * 2048:(t + 1) * 2048]:
                         e.dma_start(out=d, in_=a[:, :]), inc=SSP, k=16, dma=True)
        epi_vc[t] = VC.n
        if dbg:
            WT("gpsimd", VC, epi_vc[t])
            EM("gpsimd", lambda e, a=attnres[t], d=d_attn[:, t * 2048:(t + 1) * 2048]:
               e.dma_start(out=d, in_=a[:, :]), inc=SDBG, k=16, dma=True)
            EM("gpsimd", lambda e, a=arena[t], d=d_x1[:, t * 2048:(t + 1) * 2048]:
               e.dma_start(out=d, in_=a[:, :]), inc=SDBG, k=16, dma=True)

    for ec in range(NEC):
        p = ec % 2
        WT("tensor", AC, wo_cv[ec * 8 + 7])
        if ec == 0:
            WT("tensor", VC, yscale_vc)
            WT("tensor", PC, bout_cv[0])
        cnt = 0
        for t in range(NT):
            bk = (ec * NT + t) % 4
            WT("tensor", VC, bank_free.get(bk, 0))
            ps = P[bk][:, 0:512]
            EM("tensor", lambda e, p_=ps, r=bout_bf[0:1, ec * 512:(ec + 1) * 512]:
               e.matmul(p_, ones_rb[:, :], r, start=True, stop=False))
            for k in range(NK):
                wh = Wo[p][:, k * 512:(k + 1) * 512]
                yh = YT[k][:, t * 128:(t + 1) * 128]
                cnt = EM("tensor", lambda e, p_=ps, a=yh, b2=wh, q=(k == NK - 1):
                         e.matmul(p_, a, b2, start=False, stop=q),
                         inc=TMM if k == NK - 1 else None)
            WT("vector", TMM, cnt)
            dst = attnres[t][:, ec * 512:(ec + 1) * 512]
            if ec == 0 and t == 0:
                WT("vector", TMM, p1_mm_all)
                if dbg:
                    WT("vector", SDBG, dbg_p1)
            bank_free[bk] = EM("vector",
                               lambda e, d=dst, p_=ps, x=arena[t][:, ec * 512:(ec + 1) * 512],
                               r=ln1p[:, 2 * t + 1:2 * t + 2]:
                               e.scalar_tensor_tensor(d, x, r, p_, ALU.mult, ALU.add), inc=VC)
            if ec == NEC - 1:
                emit_epi_a(t)
                if t < 2:
                    emit_ln_stats(t, arena[t], ln3p[:, 2 * t:2 * t + 2])
                    xfv_vc[t] = EM("vector",
                                   lambda e, x=arena[t], m=ln3p[:, 2 * t:2 * t + 1],
                                   r=ln3p[:, 2 * t + 1:2 * t + 2], b=(xnb, xfb2)[t]:
                                   e.tensor_scalar(b[:, :], x[:, :], m, r,
                                                   ALU.subtract, ALU.mult), inc=VC)
        wo_group[ec] = cnt
        if ec + 2 < NEC:
            emit_wo_convert(ec + 2)
    p2_mm_all = TMM.n
    p2_evac_all = VC.n

    sdbg_epi = SDBG.n if dbg else 0

    xf_tp = {}
    xf_evac_t = {}
    for t in range(NT):
        if t >= 2:
            emit_ln_stats(t, arena[t], ln3p[:, 2 * t:2 * t + 2])
            WT("vector", TMM, xf_tp[t - 2])
            xfv_vc[t] = EM("vector", lambda e, x=arena[t], m=ln3p[:, 2 * t:2 * t + 1],
                           r=ln3p[:, 2 * t + 1:2 * t + 2], b=(xnb, xfb2)[t % 2]:
                           e.tensor_scalar(b[:, :], x[:, :], m, r, ALU.subtract, ALU.mult),
                           inc=VC)
        if t == 2:
            WT("tensor", VC, bank_free[2])
            WT("tensor", VC, bank_free[3])
        xf_tp[t] = emit_transposes((xnb, xfb2)[t % 2], t, xfv_vc[t], YT,
                                   banks=(2, 3) if t < 2 else (0, 1, 2, 3))
        xf_evac_t[t] = (AC.n, VC.n)
    xf_tp_all = TMM.n
    xf_evac_all = (AC.n, VC.n)
    if dbg:
        WT("gpsimd", AC, xf_evac_all[0])
        WT("gpsimd", VC, xf_evac_all[1])
        for k in range(NK):
            EM("gpsimd", lambda e, s=YT[k], d=d_xft[:, k * TOK:(k + 1) * TOK]:
               e.dma_start(out=d, in_=s[:, :]), inc=SDBG, k=16, dma=True)
        sdbg_xft = SDBG.n
        WT("tensor", SDBG, sdbg_xft)

    fdma = {}
    quant = {}
    quant_sig = {}
    mm12 = {}
    m3_group = {}
    hid_done = {}
    sig_done = {}
    ar_evac = {}

    fstage = [sstage[0], sstage[1], wstage[0], wstage[1]]

    def emit_fdma(n):
        if n >= 6 * HB:
            return
        hb, part = n // 6, n % 6
        m, h = part // 2, part % 2
        img = (w1_img, w2_img, w3_img)[m]
        src = img[:, hb * 2048 + h * 1024: hb * 2048 + (h + 1) * 1024]
        slotn = n % 4
        qname = ("sync", "scalar")[n % 2]
        sem = SF[n % 2]
        if n in (0, 1):
            WT(qname, AC, qcv_done[2 * NE - 1])
        if n in (2, 3):
            WT(qname, AC, wo_cv[NEC * 8 - 1])
        WT(qname, AC, quant_sig.get(n - 4, 0))
        EM(qname, lambda e, d=fstage[slotn], s=src: e.dma_start(out=d[:, :], in_=s),
           inc=sem, k=16, dma=True)
        fdma[n] = sem.n

    def emit_quant(n):
        hb, part = n // 6, n % 6
        m, h = part // 2, part % 2
        slotn = n % 4
        if m == 0:
            dst = q1r[hb % 2][:, h * 1024:(h + 1) * 1024]
        elif m == 1:
            dst = q2r[hb % 2][:, h * 1024:(h + 1) * 1024]
        else:
            dst = w3q[hb % G3][:, h * 1024:(h + 1) * 1024]
        if hb < 4:
            if dbg:
                WT("vector", SDBG, sdbg_epi)
            if m == 0:
                WT("vector", VC, epi_vc[hb % 2])
            elif m == 1:
                WT("vector", VC, epi_vc[2 + hb % 2])
            else:
                WT("vector", VC, epi_vc[4 + hb % 4])
        if m < 2 and hb >= 2:
            WT("vector", TMM, mm12.get(hb - 2, 0))
        if m == 2 and hb >= G3:
            WT("vector", TMM, m3_group.get(hb // G3 - 1, 0))
        WT("scalar", SF[n % 2], fdma[n])
        WT("scalar", VC, athr_vc[0])
        if n in (0, 3):
            WT("scalar", TMM, p2_mm_all)
        if n in (1, 2):
            WT("scalar", TMM, xf_tp_all)
        WT("scalar", VC, quant.get(n - 4, 0))
        sg = qsgn[n % 4]
        EM("scalar", lambda e, s=fstage[slotn], d=sg[:, 0:1024], b=nthr[:, m:m + 1]:
           e.sign(d, s[:, :], bias=b), inc=AC)
        quant_sig[n] = EM("scalar",
                          lambda e, s=fstage[slotn], d=sg[:, 1024:2048], b=gthr[:, m:m + 1]:
                          e.sign(d, s[:, :], bias=b), inc=AC)
        WT("vector", AC, quant_sig[n])
        quant[n] = EM(
            "vector",
            lambda e, d=dst, s=sg:
            e.tensor_tensor(d, s[:, 0:1024], s[:, 1024:2048], ALU.add),
            inc=VC)

    for n in range(2):
        emit_fdma(n)
    for hb in range(2):
        for part in range(6):
            n = 6 * hb + part
            emit_quant(n)
            emit_fdma(n + 2)

    x1rd = {}
    for t in range(2):
        WT("scalar", VC, epi_vc[NT - 1])
        WT("scalar", AC, xf_evac_all[0])
        WT("scalar", VC, xf_evac_all[1])
        src = x1_dram[:, t * 2048:(t + 1) * 2048]
        EM("scalar", lambda e, d=x1rr[t % 2], s=src: e.dma_start(out=d[:, :], in_=s),
           inc=SXR, k=16, dma=True)
        x1rd[t] = SXR.n

    def emit_p4(t):
        WT("vector", SXR, x1rd[t])
        emit_ln_stats(t, arena[t], ctmp[:, 6:8])
        for h in range(2):
            sl = slice(h * 1024, (h + 1) * 1024)
            EM("vector", lambda e, a=arena[t], s=sl:
               e.tensor_scalar(scr_a[:, :], a[:, s], ctmp[:, 6:7], ctmp[:, 7:8],
                               ALU.subtract, ALU.mult))
            EM("vector", lambda e, a=arena[t], x=x1rr[t % 2], s=sl, ng=negc2[:, t:t + 1]:
               e.scalar_tensor_tensor(a[:, s], x[:, s], ng, scr_a[:, :],
                                      ALU.add, ALU.add), inc=VC)
        yv = VC.n
        WT("scalar", VC, yv)
        EM("scalar", lambda e, a=arena[t], d=y_img[:, t * 2048:(t + 1) * 2048]:
           e.dma_start(out=d, in_=a[:, :]), inc=SYO, k=16, dma=True)
        if t + 2 < NT:
            WT("scalar", VC, yv)
            src = x1_dram[:, (t + 2) * 2048:(t + 3) * 2048]
            EM("scalar", lambda e, d=x1rr[t % 2], s=src: e.dma_start(out=d[:, :], in_=s),
               inc=SXR, k=16, dma=True)
            x1rd[t + 2] = SXR.n

    spill_gated = set()
    bankpair_last = {0: 0, 1: 0}
    for hb in range(HB):
        hs = hb % HRING
        WT("tensor", VC, quant[6 * hb + 1])
        if hb >= 1:
            WT("tensor", AC, sig_done.get(hb - 1, 0))
        cnt = 0
        for tc in range(2):
            if hb == 0:
                g_ = xf_evac_t[3] if tc == 0 else xf_evac_t[7]
                WT("tensor", AC, g_[0])
                WT("tensor", VC, g_[1])
                WT("tensor", VC, bank_free[tc])
                WT("tensor", VC, bank_free[2 + tc])
            for k in range(NK):
                a = q1r[hb % 2][:, k * 128:(k + 1) * 128]
                xh = YT[k][:, tc * 512:(tc + 1) * 512]
                cnt = EM("tensor", lambda e, p=P[tc], a_=a, b2=xh, s=(k == 0), q=(k == NK - 1):
                         e.matmul(p[:, :], a_, b2, start=s, stop=q),
                         inc=TMM if k == NK - 1 else None)
        mm1_cnt = cnt
        WT("tensor", VC, quant[6 * hb + 3])
        if hb == 0:
            WT("tensor", AC, pt_evac[0])
            WT("tensor", AC, pt_evac[1])
        if hb >= 1:
            WT("tensor", VC, hid_done.get(hb - 1, 0))
        for tc in range(2):
            for k in range(NK):
                a = q2r[hb % 2][:, k * 128:(k + 1) * 128]
                xh = YT[k][:, tc * 512:(tc + 1) * 512]
                cnt = EM("tensor", lambda e, p=P[2 + tc], a_=a, b2=xh, s=(k == 0), q=(k == NK - 1):
                         e.matmul(p[:, :], a_, b2, start=s, stop=q),
                         inc=TMM if k == NK - 1 else None)
        mm12[hb] = cnt
        WT("scalar", TMM, mm1_cnt)
        if hb >= 1:
            WT("scalar", VC, hid_done.get(hb - 1, 0))
        sg = 0
        for tc in range(2):
            sg = EM("scalar", lambda e, p=P[tc], d=scr_a[:, tc * 512:(tc + 1) * 512]:
                    e.activation(d, p[:, :], AF.Sigmoid, scale=0.5), inc=AC)
        sig_done[hb] = sg
        if hb + 1 < HB and hb + 1 >= 2:
            for part in range(4):
                n = 6 * (hb + 1) + part
                emit_quant(n)
                emit_fdma(n + 2)
        WT("vector", AC, sg)
        WT("vector", TMM, mm12[hb])
        if hb < HRING:
            WT("vector", TMM, p2_mm_all)
            if dbg:
                WT("vector", SDBG, sdbg_epi)
        for tc in range(2):
            if hb >= HRING and tc == 0:
                WT("vector", TMM, m3_group.get((hb - HRING) // G3, 0))
            hid_done[hb] = EM("vector",
                              lambda e, d=hid[hs][:, tc * 512:(tc + 1) * 512],
                              s=scr_a[:, tc * 512:(tc + 1) * 512], p=P[2 + tc]:
                              e.scalar_tensor_tensor(d, s, 0.5, p[:, :], ALU.mult, ALU.mult),
                              inc=VC)
        g = hb // G3
        if hb % G3 == G3 - 1 or hb == HB - 1:
            hbs = list(range(g * G3, min((g + 1) * G3, HB)))
            WT("tensor", VC, hid_done[hbs[-1]])
            WT("tensor", VC, quant[6 * hbs[-1] + 5])
            if g == 0:
                WT("tensor", AC, pt_evac[2])
                WT("tensor", AC, pt_evac[3])
            for eh in range(2):
                for t in range(NT):
                    pair = t % 2
                    WT("tensor", VC, bankpair_last[pair])
                    if g == 0 and t not in spill_gated:
                        WT("tensor", SSP, x1_spill[t])
                        if dbg:
                            WT("tensor", SDBG, sdbg_epi)
                        spill_gated.add(t)
                    cnt3 = 0
                    for j in range(2):
                        e_ = 2 * eh + j
                        bank = P[4 + 2 * pair + j]
                        first = True
                        for hbk in hbs:
                            hsl = hbk % HRING
                            lh = hid[hsl][:, t * 128:(t + 1) * 128]
                            rr = w3q[hbk % G3][:, e_ * 512:(e_ + 1) * 512]
                            cnt3 = EM("tensor", lambda e, p=bank, a_=lh, b2=rr, s=first,
                                      q=(hbk == hbs[-1]):
                                      e.matmul(p[:, :], a_, b2, start=s, stop=q),
                                      inc=TMM if hbk == hbs[-1] else None)
                            first = False
                    WT("vector", TMM, cnt3)
                    av = 0
                    for j in range(2):
                        e_ = 2 * eh + j
                        bank = P[4 + 2 * pair + j]
                        dst = arena[t][:, e_ * 512:(e_ + 1) * 512]
                        if g == 0:
                            av = EM("vector", lambda e, d=dst, p=bank:
                                    e.tensor_scalar(d, p[:, :], 0.5, None, ALU.mult), inc=VC)
                        else:
                            av = EM("vector", lambda e, d=dst, p=bank:
                                    e.scalar_tensor_tensor(d, p[:, :], 0.5, d, ALU.mult, ALU.add),
                                    inc=VC)
                    bankpair_last[pair] = av
                    ar_evac[(g, t)] = max(ar_evac.get((g, t), 0), av)
                    if g == (HB - 1) // G3 and eh == 1:
                        emit_p4(t)
            m3_group[g] = cnt3
        if hb + 1 < HB and hb + 1 >= 2:
            for part in (4, 5):
                n = 6 * (hb + 1) + part
                emit_quant(n)
                emit_fdma(n + 2)

    last_g = (HB - 1) // G3
    if dbg:
        for t in range(NT):
            WT("gpsimd", VC, ar_evac[(last_g, t)])
            EM("gpsimd", lambda e, a=arena[t], d=d_ffn[:, t * 2048:(t + 1) * 2048]:
               e.dma_start(out=d, in_=a[:, :]), inc=SDBG, k=16, dma=True)

    WT("scalar", SYO, SYO.n)
    WT("scalar", SSP, SSP.n)
    if dbg:
        WT("gpsimd", SDBG, SDBG.n)

    with nc.Block() as blk:
        @blk.tensor
        def _(eng):
            for f in prog["tensor"]:
                f(eng)

        @blk.vector
        def _(eng):
            for f in prog["vector"]:
                f(eng)

        @blk.scalar
        def _(eng):
            for f in prog["scalar"]:
                f(eng)

        @blk.gpsimd
        def _(eng):
            for f in prog["gpsimd"]:
                f(eng)

        @blk.sync
        def _(eng):
            for f in prog["sync"]:
                f(eng)

    return nc


def _permD(n):
    p = np.arange(n)
    p[:1024] = np.concatenate([np.arange(0, 1024, 2), np.arange(1, 1024, 2)])
    return p


def _prep_inputs(X, W_in, b_in, W_out, b_out, w1, w2, w3):
    f32 = np.float32
    pD = _permD(DM)
    order = np.concatenate([2048 + pD, 4096 + pD, 0 + pD])
    WtIn = np.ascontiguousarray(W_in[order, :].T, dtype=f32)
    wqkv = np.ascontiguousarray(WtIn.reshape(NK, 128, NE, 128).transpose(1, 2, 0, 3).reshape(128, -1))
    bq = np.ascontiguousarray(np.asarray(b_in, dtype=f32)[order].reshape(NE, 128).T)
    WoT = np.ascontiguousarray(np.asarray(W_out, dtype=f32).T[pD, :])
    wout = np.ascontiguousarray(WoT.reshape(NK, 128, NEC, 512).transpose(1, 2, 0, 3).reshape(128, -1))
    boutr = np.ascontiguousarray(np.asarray(b_out, dtype=f32).reshape(1, DM))
    w1T = np.ascontiguousarray(np.asarray(w1, dtype=f32).T)
    w2T = np.ascontiguousarray(np.asarray(w2, dtype=f32).T)
    w3T = np.ascontiguousarray(np.asarray(w3, dtype=f32).T)
    w1i = np.ascontiguousarray(w1T.reshape(NK, 128, HB, 128).transpose(1, 2, 0, 3).reshape(128, -1))
    w2i = np.ascontiguousarray(w2T.reshape(NK, 128, HB, 128).transpose(1, 2, 0, 3).reshape(128, -1))
    w3i = np.ascontiguousarray(w3T.reshape(HB, 128, 4, 512).transpose(1, 0, 2, 3).reshape(128, -1))
    eye = np.eye(128, dtype=ml_dtypes.bfloat16)

    invf = 10000.0 ** (-(np.arange(512, dtype=np.float64) * 2.0) / 1024.0)
    xs = np.asarray(X, dtype=f32).reshape(B * S, DM)

    in_maps = []
    GW = NGS * 1024 // 3
    for c in range(NCORE):
        rows = xs[c * TOK:(c + 1) * TOK]
        ximg = np.ascontiguousarray(rows.reshape(NT, 128, DM).transpose(1, 0, 2).reshape(128, -1))
        p0 = (c % 4) * TOK
        ang = np.arange(p0, p0 + TOK, dtype=np.float64)[None, :] * invf[:, None]
        cosi = np.ascontiguousarray(np.cos(ang).astype(f32).reshape(4, 128, TOK)
                                    .transpose(1, 0, 2).reshape(128, -1))
        sini = np.ascontiguousarray(np.sin(ang).astype(f32).reshape(4, 128, TOK)
                                    .transpose(1, 0, 2).reshape(128, -1))
        bmask = np.zeros((128, 32), dtype=f32)
        bmask[:, 16 * (c // 4): 16 * (c // 4) + 16] = 1.0
        gsl = np.ascontiguousarray(np.concatenate(
            [w1i[:, c * GW:(c + 1) * GW], w2i[:, c * GW:(c + 1) * GW],
             w3i[:, c * GW:(c + 1) * GW]], axis=1))
        in_maps.append({
            "x_img": ximg, "wqkv_img": wqkv, "bqkv_img": bq, "wout_img": wout,
            "bout_img": boutr, "w1_img": w1i, "w2_img": w2i, "w3_img": w3i,
            "gslab_img": gsl, "cos_img": cosi, "sin_img": sini,
            "eye_in": eye, "bmask_in": bmask,
        })
    return in_maps


def kernel(L0_attn_X, L0_attn_W_in, L0_attn_b_in, L0_attn_W_out, L0_attn_b_out,
           L0_ffn_w1, L0_ffn_w2, L0_ffn_w3):
    in_maps = _prep_inputs(np.asarray(L0_attn_X), np.asarray(L0_attn_W_in),
                           np.asarray(L0_attn_b_in), np.asarray(L0_attn_W_out),
                           np.asarray(L0_attn_b_out), np.asarray(L0_ffn_w1),
                           np.asarray(L0_ffn_w2), np.asarray(L0_ffn_w3))
    nc = build_nc()
    res = run_bass_kernel_spmd(nc, in_maps, core_ids=list(range(NCORE)))
    parts = []
    for c in range(NCORE):
        yi = np.asarray(res.results[c]["y_img"])
        parts.append(yi.reshape(128, NT, DM).transpose(1, 0, 2).reshape(TOK, DM))
    return np.concatenate(parts, axis=0).reshape(B, S, DM).astype(np.float32)
